# version 08
# baseline (speedup 1.0000x reference)
import numpy as np
from contextlib import ExitStack

import concourse.bass as bass
import concourse.mybir as mybir

F32 = mybir.dt.float32
BF16 = mybir.dt.bfloat16
AF = mybir.ActivationFunctionType
OP = mybir.AluOpType
P = 128
D = 1024
NFULL = 4096
NT = 32
NOWN = 2048
HEADS = 16
HD = 64

ENGS = ("sync", "tensor", "vector", "scalar", "gpsimd")


class Sem:
    __slots__ = ("h", "n", "name")

    def __init__(self, nc, ctx, name):
        self.h = ctx.enter_context(nc.semaphore(name))
        self.n = 0
        self.name = name


class Builder:
    def __init__(self, nc, esem):
        self.nc = nc
        self.esem = esem
        self.steps = {e: [] for e in ENGS}
        self.phase = "init"
        self.mark = {e: {} for e in ENGS}
        self.wr = {}
        self.rd = {}

    def wait(self, eng, sem, val=None):
        v = sem.n if val is None else val
        if v <= 0 or self.mark[eng].get(sem, 0) >= v:
            return
        self.mark[eng][sem] = v
        self.steps[eng].append((self.phase, lambda e, s=sem.h, v=v: e.wait_ge(s, v)))

    def _keys(self, h, tag):
        base = id(h) if not isinstance(h, str) else h
        return (base, tag), (base, None)

    def op(self, eng, fn, reads=(), writes=(), sem=None, inc=1, extra_inc=()):
        def norm(lst):
            out = []
            for it in lst:
                if isinstance(it, tuple):
                    out.append(it)
                else:
                    out.append((it, None))
            return out
        reads, writes = norm(reads), norm(writes)
        for h, tag in reads:
            for key in self._keys(h, tag) if tag is not None else [(id(h) if not isinstance(h, str) else h, None)]:
                w = self.wr.get(key)
                if w:
                    self.wait(eng, w[0], w[1])
            if tag is None:
                base = id(h) if not isinstance(h, str) else h
                for key, w in list(self.wr.items()):
                    if key[0] == base and key[1] is not None:
                        self.wait(eng, w[0], w[1])
        for h, tag in writes:
            base = id(h) if not isinstance(h, str) else h
            keys = [(base, tag)] if tag is not None else [k for k in set(list(self.wr) + list(self.rd)) if k[0] == base] + [(base, None)]
            if tag is not None:
                keys.append((base, None))
            for key in set(keys):
                w = self.wr.get(key)
                if w:
                    self.wait(eng, w[0], w[1])
                for s, v in self.rd.get(key, {}).items():
                    self.wait(eng, s, v)
        s = sem or self.esem[eng]
        s.n += inc
        val = s.n
        for es, k in extra_inc:
            es.n += k
        incs = tuple([(s, inc)] + list(extra_inc))

        def wrapped(e, fn=fn, incs=incs):
            inst = fn(e)
            for ss, k in incs:
                inst.then_inc(ss.h, k)
        self.steps[eng].append((self.phase, wrapped))
        for h, tag in writes:
            base = id(h) if not isinstance(h, str) else h
            if tag is None:
                for key in [k for k in list(self.wr) if k[0] == base]:
                    del self.wr[key]
                for key in [k for k in list(self.rd) if k[0] == base]:
                    del self.rd[key]
            self.wr[(base, tag)] = (s, val)
            self.rd.pop((base, tag), None)
        for h, tag in reads:
            base = id(h) if not isinstance(h, str) else h
            self.rd.setdefault((base, tag), {})[s] = val
        return (s, val)

    def barrier(self, extra_sems=()):
        sems = list(self.esem.values()) + list(extra_sems)
        for eng in ENGS:
            for s in sems:
                self.wait(eng, s, s.n)
        self.wr.clear()
        self.rd.clear()

    def emit(self):
        nc = self.nc

        def run(e, steps):
            i = 0
            while i < len(steps):
                ph = steps[i][0]
                j = i
                with nc.named_scope(ph):
                    while j < len(steps) and steps[j][0] == ph:
                        steps[j][1](e)
                        j += 1
                i = j
        with self.nc.Block() as blk:
            for eng in ENGS:
                steps = self.steps[eng]
                if not steps:
                    continue
                getattr(blk, eng)(lambda e, steps=steps: run(e, steps))


class MK:
    def __init__(self, cfg):
        self.cfg = cfg
        self.nc = bass.Bass("TRN2", target_bir_lowering=False)
        self.ctx = ExitStack()
        nc, ctx = self.nc, self.ctx
        esem = {e: Sem(nc, ctx, "c_" + e) for e in ENGS}
        self.sem_pool = [Sem(nc, ctx, f"d{i}") for i in range(24)]
        self._sem_i = 0
        self.B = Builder(nc, esem)
        self.dram = {}
        self.t = {}

    def newsem(self):
        s = self.sem_pool[self._sem_i]
        self._sem_i += 1
        return s

    def barrier(self):
        self.B.barrier([s for s in self.sem_pool if s.n > 0])

    def seal_group(self, sem, handles):
        for h in handles:
            if isinstance(h, tuple):
                base = id(h[0]) if not isinstance(h[0], str) else h[0]
                key = (base, h[1])
            else:
                key = (id(h) if not isinstance(h, str) else h, None)
            self.B.wr[key] = (sem, sem.n)

    def dramt(self, name, shape, dtype=F32, kind="ExternalInput"):
        h = self.nc.dram_tensor(name, list(shape), dtype, kind=kind)
        self.dram[name] = h.ap()
        return self.dram[name]

    def sb(self, name, shape, dtype, ctx=None, side=None):
        if side:
            h = (ctx or self.ctx).enter_context(
                self.nc.sbuf_tensor(name, list(shape), dtype, side=side))
        else:
            h = (ctx or self.ctx).enter_context(
                self.nc.sbuf_tensor(name, list(shape), dtype))
        self.t[name] = h
        return h

    def ps(self, name, shape, dtype, ctx=None):
        h = (ctx or self.ctx).enter_context(
            self.nc.psum_tensor(name, list(shape), dtype))
        self.t[name] = h
        return h

    def load(self, eng, dst_ap, src_ap, bufkey, sem):
        self.B.op(eng, lambda e, d=dst_ap, s=src_ap: e.dma_start(out=d, in_=s),
                  writes=[bufkey], sem=sem, inc=16)

    def declare_io(self):
        d = self.dramt
        d("x", (NFULL, D))
        d("wqkvT", (D, 3 * D))
        d("woutT", (D, D))
        d("b_out", (1, D))
        for i, (a, b) in enumerate([(D, D), (D, 512), (512, 256), (256, 1)]):
            if b > 1:
                d(f"cn_w{i+1}", (a, b))
            d(f"cn_b{i+1}", (1, b))
        d("cn_w4c", (P, 2))
        d("gc_w1", (D, 512))
        d("gc_w2", (512, 256))
        d("gc_w3c", (P, 2))
        d("gc_b1c", (P, 4))
        d("gc_b2c", (P, 2))
        d("gc_b3", (1, 1))
        d("ar", (1, 1))
        for w in ("Wq", "Wk", "Wv"):
            d(w, (8, D, P))
        d("Wo", (D, D))
        for b in ("bq", "bk"):
            d(b, (P, 8))
        d("bv_row", (1, D))
        d("bo_row", (1, D))
        d("norm_scale", (1, D))
        d("gate_w", (D, 8))
        d("gate_b", (8, 1))
        d("mlp1_w", (8, D, 4096))
        d("mlp1_b", (8, 4096))
        d("mlp2_w", (8, 2048, D))
        d("mlp2_b", (8, D))
        d("out", (NOWN, D), kind="ExternalOutput")
        for tap in ("tap_xmid", "tap_l0"):
            if self.cfg.get(tap):
                d("dbg_" + tap[4:], (NFULL, D), kind="ExternalOutput")
        if self.cfg.get("tap_l1a"):
            d("dbg_l1a", (NOWN, D), kind="ExternalOutput")
        if self.cfg.get("tap_moe"):
            d("dbg_w8", (P, 16, 8), kind="ExternalOutput")
            d("dbg_yacc", (P, 16, D), kind="ExternalOutput")
        if self.cfg.get("tap_sel"):
            d("dbg_raw0", (P, NFULL), kind="ExternalOutput")
            d("dbg_raw1", (P, NFULL), kind="ExternalOutput")
            d("dbg_selc", (P, 48), kind="ExternalOutput")
            d("dbg_den8", (P, 8), kind="ExternalOutput")
            d("dbg_w0", (P, NFULL), kind="ExternalOutput")
            d("dbg_w1", (P, NFULL), kind="ExternalOutput")
        if self.cfg.get("tap_atn") is not None:
            d("dbg_atnT", (P, 8, NOWN // 2), kind="ExternalOutput")

    def setup(self):
        B, t = self.B, self.t
        self.B.phase = "setup"
        self.sb("x_own", (P, 16, D), F32)
        self.ctx_oth = ExitStack()
        self.sb("x_oth", (P, 16, D), BF16, self.ctx_oth, side="right")
        self.sb("ident", (P, P), BF16)
        self.sb("identf", (P, P), F32)
        self.sb("ones_row", (1, P), BF16)
        self.sb("zero_row", (1, 512), BF16)
        self.sb("onecol_bf", (P, 1), BF16)
        self.sb("negm", (P, NT), F32)
        self.sb("rstd", (P, NT), F32)
        self.sb("stat", (P, 16), F32)
        self.sb("epscol", (P, 1), F32)

        x = self.dram["x"]
        half = self.cfg["half"]
        xr = x.rearrange("(i p) d -> p i d", p=P)
        self.load("sync", t["x_own"][:, :, :],
                  xr[:, 16 * half:16 * half + 16, :], t["x_own"], self.newsem())
        self.load("gpsimd", t["x_oth"][:, :, :],
                  xr[:, 16 * (1 - half):16 * (1 - half) + 16, :], t["x_oth"],
                  self.newsem())
        for name, v in [("epscol", 1e-5), ("ones_row", 1.0),
                        ("zero_row", 0.0), ("onecol_bf", 1.0)]:
            B.op("gpsimd", lambda e, n=name, v=v: e.memset(t[n][:], v),
                 writes=[t[name]])
        for name in ("ident", "identf"):
            B.op("gpsimd", lambda e, n=name: e.memset(t[n][:], 0.0),
                 writes=[t[name]])
            B.op("gpsimd", lambda e, n=name: e.affine_select(
                out=t[n][:], in_=t[n][:], compare_op=OP.not_equal,
                fill=1.0, base=0, pattern=[[-1, P]], channel_multiplier=1),
                reads=[t[name]], writes=[t[name]])

    def xtile(self, i):
        half = self.cfg["half"]
        lo = 16 * half
        if lo <= i < lo + 16:
            return self.t["x_own"][:, i - lo, :], (self.t["x_own"], i - lo)
        j = i - 16 * (1 - half)
        return self.t["x_oth"][:, j, :], (self.t["x_oth"], j)

    def ln_stats(self, i, negm, rstd):
        B, t = self.B, self.t
        xa, xkey = self.xtile(i)
        c0 = 8 * (i % 2)
        st = t["stat"]
        B.op("scalar", lambda e, xa=xa, c0=c0, jk=self.junk: e.activation(
            jk[:], xa, AF.Square, accum_out=st[:, c0:c0 + 1]),
            reads=[xkey], writes=[(st, c0), self.junk])
        B.op("scalar", lambda e, xa=xa, c0=c0, jk=self.junk: e.activation(
            jk[:], xa, AF.Identity, accum_out=st[:, c0 + 1:c0 + 2]),
            reads=[xkey], writes=[(st, c0 + 1), self.junk])
        B.op("vector", lambda e, c0=c0, i=i: e.tensor_scalar(
            out=negm[:, i:i + 1], in0=st[:, c0 + 1:c0 + 2], scalar1=-1.0 / D,
            scalar2=None, op0=OP.mult),
            reads=[(st, c0 + 1)], writes=[(negm, i)])
        B.op("vector", lambda e, c0=c0, i=i: e.tensor_tensor(
            out=st[:, c0 + 2:c0 + 3], in0=negm[:, i:i + 1], in1=negm[:, i:i + 1],
            op=OP.mult), reads=[(negm, i)], writes=[(st, c0 + 2)])
        B.op("vector", lambda e, c0=c0: e.tensor_scalar(
            out=st[:, c0 + 3:c0 + 4], in0=st[:, c0:c0 + 1], scalar1=1.0 / D,
            scalar2=None, op0=OP.mult),
            reads=[(st, c0)], writes=[(st, c0 + 3)])
        B.op("vector", lambda e, c0=c0: e.tensor_tensor(
            out=st[:, c0 + 4:c0 + 5], in0=st[:, c0 + 3:c0 + 4],
            in1=st[:, c0 + 2:c0 + 3], op=OP.subtract),
            reads=[(st, c0 + 3), (st, c0 + 2)], writes=[(st, c0 + 4)])
        B.op("scalar", lambda e, c0=c0: e.activation(
            st[:, c0 + 5:c0 + 6], st[:, c0 + 4:c0 + 5], AF.Sqrt,
            bias=t["epscol"][:, 0:1]),
            reads=[(st, c0 + 4), t["epscol"]], writes=[(st, c0 + 5)])
        B.op("vector", lambda e, c0=c0, i=i: e.reciprocal(
            rstd[:, i:i + 1], st[:, c0 + 5:c0 + 6]),
            reads=[(st, c0 + 5)], writes=[(rstd, i)])

    def ln_apply(self, i, negm, rstd, out_bf):
        B = self.B
        xa, xkey = self.xtile(i)
        B.op("vector", lambda e, xa=xa, i=i: e.tensor_scalar(
            out=out_bf[:], in0=xa, scalar1=negm[:, i:i + 1],
            scalar2=rstd[:, i:i + 1], op0=OP.add, op1=OP.mult),
            reads=[xkey, (negm, i), (rstd, i)], writes=[out_bf])

    def transpose_tile(self, src, tp_ps, dst, nblk=8):
        B, t = self.B, self.t

        def f(e, src=src, tp=tp_ps, nblk=nblk):
            for c in range(nblk):
                inst = e.transpose(tp[:, c * P:(c + 1) * P],
                                   src[:, c * P:(c + 1) * P], t["ident"][:])
            return inst
        B.op("tensor", f, reads=[src, t["ident"]], writes=[tp_ps])
        B.op("vector", lambda e, tp=tp_ps, dst=dst: e.tensor_copy(dst[:, :, :], tp[:]),
             reads=[tp_ps], writes=[dst])

    def phase_A(self):
        B, t = self.B, self.t
        self.B.phase = "A"
        ctxAB = self.ctxAB = ExitStack()
        self.sb("wqkvT_sb", (P, 8, 3 * D), BF16, ctxAB)
        self.sb("woutT_sb", (P, 8, D), BF16, ctxAB)
        self.sb("kv_sb", (P, D), BF16, ctxAB)
        self.sb("ksumB", (P, 8, 16), BF16, ctxAB)
        self.sb("bout_bf", (1, D), BF16, ctxAB)

        self.load("gpsimd", t["wqkvT_sb"][:, :, :],
                  self.dram["wqkvT"].rearrange("(c p) o -> p c o", p=P),
                  t["wqkvT_sb"], self.newsem())
        self.load("gpsimd", t["woutT_sb"][:, :, :],
                  self.dram["woutT"].rearrange("(c p) o -> p c o", p=P),
                  t["woutT_sb"], self.newsem())
        self.load("gpsimd", t["bout_bf"][:], self.dram["b_out"][:],
                  t["bout_bf"], self.newsem())

        ctxA = ExitStack()
        self.junk = self.sb("junkA", (P, D), BF16, ctxA)
        xn = [self.sb(f"a_xn{p}", (P, D), BF16, ctxA) for p in range(2)]
        xnT = [self.sb(f"a_xnT{p}", (P, 8, P), BF16, ctxA) for p in range(2)]
        abuf = [self.sb(f"a_ab{p}", (P, D), BF16, ctxA) for p in range(2)]
        ebuf = [self.sb(f"a_eb{p}", (P, D), BF16, ctxA) for p in range(2)]
        kpr = [self.sb(f"a_kp{p}", (P, D), BF16, ctxA) for p in range(2)]
        vbf = [self.sb(f"a_vb{p}", (P, D), BF16, ctxA) for p in range(2)]
        tp_ps = self.ps("a_tp", (P, D), BF16, ctxA)
        kvproj = self.ps("a_kvproj", (P, 2048), F32, ctxA)
        kv_ps = self.ps("a_kv", (P, D), F32, ctxA)

        def opener(e):
            e.matmul(kv_ps[:, 0:512], t["zero_row"][:, 0:P], t["zero_row"][:],
                     start=True, stop=False)
            return e.matmul(kv_ps[:, 512:1024], t["zero_row"][:, 0:P],
                            t["zero_row"][:], start=True, stop=False)
        B.op("tensor", opener, reads=[t["zero_row"]], writes=[(kv_ps, "open")])

        for i in range(NT):
            par = i % 2
            self.ln_stats(i, t["negm"], t["rstd"])
            self.ln_apply(i, t["negm"], t["rstd"], xn[par])
            self.transpose_tile(xn[par], tp_ps, xnT[par])

            def kvproj_mm(e, par=par):
                for nb in range(4):
                    for c in range(8):
                        inst = e.matmul(
                            kvproj[:, nb * 512:(nb + 1) * 512],
                            xnT[par][:, c, :],
                            t["wqkvT_sb"][:, c, D + nb * 512:D + (nb + 1) * 512],
                            start=(c == 0), stop=(c == 7))
                return inst
            B.op("tensor", kvproj_mm, reads=[xnT[par], t["wqkvT_sb"]],
                 writes=[kvproj])
            B.op("vector", lambda e, par=par: e.tensor_scalar(
                out=abuf[par][:], in0=kvproj[:, 0:D], scalar1=0.0, scalar2=None,
                op0=OP.min), reads=[kvproj], writes=[abuf[par]])
            B.op("scalar", lambda e, par=par: e.activation(
                ebuf[par][:], abuf[par][:], AF.Exp),
                reads=[abuf[par]], writes=[ebuf[par]])
            B.op("vector", lambda e, par=par: e.scalar_tensor_tensor(
                out=kpr[par][:], in0=kvproj[:, 0:D], scalar=0.0, in1=ebuf[par][:],
                op0=OP.max, op1=OP.add),
                reads=[kvproj, ebuf[par]], writes=[kpr[par]])
            B.op("vector", lambda e, par=par: e.tensor_copy(
                vbf[par][:], kvproj[:, D:2 * D]),
                reads=[kvproj], writes=[vbf[par]])

            def kv_mm(e, par=par):
                for h in range(HEADS):
                    rp = 64 * (h % 2)
                    cp = 128 * (h // 2)
                    e.matmul(kv_ps[rp:rp + 64, cp:cp + 64],
                             kpr[par][:, 64 * h:64 * h + 64],
                             vbf[par][:, 64 * h:64 * h + 64],
                             start=False, stop=False, tile_position=(0, rp))
                    inst = e.matmul(kv_ps[rp:rp + 64, cp + 64:cp + 65],
                                    kpr[par][:, 64 * h:64 * h + 64],
                                    t["onecol_bf"][:],
                                    start=False, stop=False, tile_position=(0, rp))
                return inst
            B.op("tensor", kv_mm, reads=[kpr[par], vbf[par], t["onecol_bf"],
                                         (kv_ps, "open")], writes=[(kv_ps, i)])

        def closer(e):
            e.matmul(kv_ps[:, 0:512], t["zero_row"][:, 0:P], t["zero_row"][:],
                     start=False, stop=True)
            return e.matmul(kv_ps[:, 512:1024], t["zero_row"][:, 0:P],
                            t["zero_row"][:], start=False, stop=True)
        B.op("tensor", closer, reads=[t["zero_row"]], writes=[kv_ps])
        B.op("vector", lambda e: e.tensor_copy(t["kv_sb"][:], kv_ps[:]),
             reads=[kv_ps], writes=[t["kv_sb"]])
        B.op("vector", lambda e: e.memset(t["ksumB"][:], 0.0),
             writes=[t["ksumB"]])

        def ksum_copy(e):
            for h in range(HEADS):
                rp = 64 * (h % 2)
                cp = 128 * (h // 2)
                inst = e.tensor_copy(
                    self.t["ksumB"][rp:rp + 64, h // 2, h:h + 1],
                    kv_ps[rp:rp + 64, cp + 64:cp + 65])
            return inst
        B.op("vector", ksum_copy, reads=[kv_ps, t["ksumB"]], writes=[t["ksumB"]])
        self.barrier()
        ctxA.close()

    def phase_B(self):
        B, t = self.B, self.t
        self.B.phase = "B"
        ctxB = ExitStack()
        self.sb("selz", (16, 16, 64), BF16, ctxB)
        B.op("gpsimd", lambda e: e.memset(t["selz"][:, :, :], 0.0),
             writes=[t["selz"]])

        def selz_fill(e):
            for h in range(16):
                inst = e.affine_select(
                    out=t["selz"][:, h, :], in_=t["selz"][:, h, :],
                    compare_op=OP.not_equal, fill=1.0, base=-h,
                    pattern=[[0, 64]], channel_multiplier=1)
            return inst
        B.op("gpsimd", selz_fill, reads=[t["selz"]], writes=[t["selz"]])
        self.junk = self.sb("junkB", (P, D), BF16, ctxB)
        xn = [self.sb(f"b_xn{p}", (P, D), BF16, ctxB) for p in range(2)]
        xnT = [self.sb(f"b_xnT{p}", (P, 8, P), BF16, ctxB) for p in range(2)]
        abuf = [self.sb("b_ab0", (P, 8, P), BF16, ctxB)] * 2
        ebuf = [self.sb("b_eb0", (P, 8, P), BF16, ctxB)] * 2
        qpr = [self.sb(f"b_qp{p}", (P, 8, P), BF16, ctxB) for p in range(2)]
        atn = [self.sb(f"b_at{p}", (P, 8, P), BF16, ctxB) for p in range(2)]
        zsb = [self.sb(f"b_z{p}", (16, P), BF16, ctxB) for p in range(2)]
        ztmp = self.sb("b_ztmp", (16, P), F32, ctxB)
        zbsb = [self.sb("b_zbs0", (P, 8, P), BF16, ctxB)] * 2
        tp_ps = self.ps("b_tp", (P, D), BF16, ctxB)
        psA = self.ps("b_psA", (P, 8, P), F32, ctxB)
        zpre = self.ps("b_zpre", (P, 512), F32, ctxB)
        zb_ps = self.ps("b_zb", (P, 8, P), F32, ctxB)
        y_ps = self.ps("b_y", (P, D), F32, ctxB)

        for i in range(NT):
            par = i % 2
            self.ln_apply(i, t["negm"], t["rstd"], xn[par])
            self.transpose_tile(xn[par], tp_ps, xnT[par])

            def qT_mm(e, par=par):
                for qc in range(8):
                    for c in range(8):
                        inst = e.matmul(
                            psA[:, qc, :],
                            t["wqkvT_sb"][:, c, qc * P:(qc + 1) * P],
                            xnT[par][:, c, :],
                            start=(qc % 4 == 0 and c == 0),
                            stop=(qc % 4 == 3 and c == 7))
                return inst
            B.op("tensor", qT_mm, reads=[xnT[par], t["wqkvT_sb"]], writes=[psA])
            B.op("vector", lambda e, par=par: e.tensor_scalar(
                out=abuf[par][:, :, :], in0=psA[:, :, :], scalar1=0.0,
                scalar2=None, op0=OP.min), reads=[psA], writes=[abuf[par]])
            B.op("scalar", lambda e, par=par: e.activation(
                ebuf[par][:, :, :], abuf[par][:, :, :], AF.Exp),
                reads=[abuf[par]], writes=[ebuf[par]])
            B.op("vector", lambda e, par=par: e.scalar_tensor_tensor(
                out=qpr[par][:, :, :], in0=psA[:, :, :], scalar=0.0,
                in1=ebuf[par][:, :, :], op0=OP.max, op1=OP.add),
                reads=[psA, ebuf[par]], writes=[qpr[par]])

            def zpre_mm(e, par=par):
                for c in range(8):
                    inst = e.matmul(zpre[0:16, 0:P], t["ksumB"][:, c, :],
                                    qpr[par][:, c, :],
                                    start=(c == 0), stop=(c == 7))
                return inst
            B.op("tensor", zpre_mm, reads=[qpr[par], t["ksumB"]], writes=[zpre])
            B.op("vector", lambda e: e.tensor_scalar(
                out=ztmp[:], in0=zpre[0:16, 0:P], scalar1=1e-6, scalar2=None,
                op0=OP.add), reads=[zpre], writes=[ztmp])
            def zrec(e, par=par):
                with self.nc.allow_low_precision(reason="z broadcast in bf16"):
                    return e.reciprocal(zsb[par][:], ztmp[:])
            B.op("vector", zrec, reads=[ztmp], writes=[zsb[par]])

            def zb_mm(e, par=par):
                e.matmul(zb_ps[:, 0:4, :], t["zero_row"][:, 0:P],
                         t["zero_row"][:], start=True, stop=False)
                e.matmul(zb_ps[:, 4:8, :], t["zero_row"][:, 0:P],
                         t["zero_row"][:], start=True, stop=False)
                for h in range(HEADS):
                    rp = 64 * (h % 2)
                    inst = e.matmul(
                        zb_ps[rp:rp + 64, h // 2, :],
                        t["selz"][:, h, :],
                        zsb[par][:, :],
                        start=False, stop=False,
                        tile_position=(0, rp))
                e.matmul(zb_ps[:, 0:4, :], t["zero_row"][:, 0:P],
                         t["zero_row"][:], start=False, stop=True)
                inst = e.matmul(zb_ps[:, 4:8, :], t["zero_row"][:, 0:P],
                                t["zero_row"][:], start=False, stop=True)
                return inst
            B.op("tensor", zb_mm, reads=[zsb[par], t["selz"]], writes=[zb_ps])
            B.op("vector", lambda e, par=par: e.tensor_copy(
                zbsb[par][:, :, :], zb_ps[:, :, :]),
                reads=[zb_ps], writes=[zbsb[par]])

            def o_mm(e, par=par):
                e.matmul(psA[:, 0:4, :], t["zero_row"][:, 0:P],
                         t["zero_row"][:], start=True, stop=False)
                e.matmul(psA[:, 4:8, :], t["zero_row"][:, 0:P],
                         t["zero_row"][:], start=True, stop=False)
                for h in range(HEADS):
                    rp = 64 * (h % 2)
                    inst = e.matmul(
                        psA[rp:rp + 64, h // 2, :],
                        t["kv_sb"][rp:rp + 64, 128 * (h // 2):128 * (h // 2) + 64],
                        qpr[par][rp:rp + 64, h // 2, :],
                        start=False, stop=False,
                        tile_position=(rp, rp))
                e.matmul(psA[:, 0:4, :], t["zero_row"][:, 0:P],
                         t["zero_row"][:], start=False, stop=True)
                inst = e.matmul(psA[:, 4:8, :], t["zero_row"][:, 0:P],
                                t["zero_row"][:], start=False, stop=True)
                return inst
            B.op("tensor", o_mm, reads=[t["kv_sb"], qpr[par]], writes=[psA])
            B.op("vector", lambda e, par=par: e.tensor_tensor(
                out=atn[par][:, :, :], in0=psA[:, :, :], in1=zbsb[par][:, :, :],
                op=OP.mult), reads=[psA, zbsb[par]], writes=[atn[par]])

            def y_mm(e, par=par):
                for oc in range(2):
                    e.matmul(y_ps[:, oc * 512:(oc + 1) * 512], t["ones_row"][:],
                             t["bout_bf"][:, oc * 512:(oc + 1) * 512],
                             start=True, stop=False)
                    for c in range(8):
                        inst = e.matmul(
                            y_ps[:, oc * 512:(oc + 1) * 512],
                            atn[par][:, c, :],
                            t["woutT_sb"][:, c, oc * 512:(oc + 1) * 512],
                            start=False, stop=(c == 7))
                return inst
            B.op("tensor", y_mm,
                 reads=[atn[par], t["woutT_sb"], t["ones_row"], t["bout_bf"]],
                 writes=[y_ps])
            xa, xkey = self.xtile(i)
            B.op("vector", lambda e, xa=xa: e.tensor_tensor(
                out=xa, in0=xa, in1=y_ps[:], op=OP.add),
                reads=[xkey, y_ps], writes=[xkey])

        self.barrier()
        ctxB.close()
        if self.cfg.get("tap_xmid"):
            self.tap_stream("dbg_xmid")

    def phase_C(self):
        B, t = self.B, self.t
        self.B.phase = "C"
        self.barrier()
        self.ctxAB.close()
        ctxC = ExitStack()
        self.junk = self.sb("junkC", (P, D), BF16, ctxC)
        self.sb("cn1_sb", (P, 8, D), BF16, ctxC)
        self.sb("cn2_sb", (P, 8, 512), BF16, ctxC)
        self.sb("cn3_sb", (P, 4, 256), BF16, ctxC)
        self.sb("cn4_sb", (P, 2), BF16, ctxC)
        self.sb("gc1_sb", (P, 8, 512), BF16, ctxC)
        self.sb("gc2_sb", (P, 4, 256), BF16, ctxC)
        self.sb("gc3_sb", (P, 2), BF16, ctxC)
        self.sb("cnb_rows", (1, 2048), BF16, ctxC)
        self.sb("gcb1c", (P, 4), F32, ctxC)
        self.sb("gcb2c", (P, 2), F32, ctxC)
        self.sb("gcb3", (1, 1), F32, ctxC)
        self.sb("cnb4", (1, 1), F32, ctxC)
        self.sb("ar_sb", (1, 1), F32, ctxC)
        self.sb("cst", (1, 8), F32, ctxC)
        self.sb("gfb", (1, 1), F32, ctxC)
        self.sb("mean_bf", (P, 8), BF16, ctxC)
        self.sb("h1g_bf", (P, 4), BF16, ctxC)
        self.sb("h2g_bf", (P, 2), BF16, ctxC)

        wsem = self.newsem()
        for nm, src in [("cn1_sb", "cn_w1"), ("cn2_sb", "cn_w2"),
                        ("gc1_sb", "gc_w1")]:
            self.load("gpsimd", t[nm][:, :, :],
                      self.dram[src].rearrange("(c p) o -> p c o", p=P),
                      t[nm], wsem)
        for nm, src, nc_ in [("cn3_sb", "cn_w3", 4), ("gc2_sb", "gc_w2", 4)]:
            self.load("gpsimd", t[nm][:, :, :],
                      self.dram[src].rearrange("(c p) o -> p c o", p=P),
                      t[nm], wsem)
        for nm, src in [("cn4_sb", "cn_w4c"), ("gc3_sb", "gc_w3c")]:
            self.load("gpsimd", t[nm][:, :], self.dram[src][:], t[nm], wsem)
        self.load("gpsimd", t["cnb_rows"][:, 0:D], self.dram["cn_b1"][:],
                  (t["cnb_rows"], 1), wsem)
        self.load("gpsimd", t["cnb_rows"][:, D:D + 512], self.dram["cn_b2"][:],
                  (t["cnb_rows"], 2), wsem)
        self.load("gpsimd", t["cnb_rows"][:, D + 512:D + 768],
                  self.dram["cn_b3"][:], (t["cnb_rows"], 3), wsem)
        sy = self.newsem()
        self.load("sync", t["gcb1c"][:], self.dram["gc_b1c"][:], t["gcb1c"], sy)
        self.load("sync", t["gcb2c"][:], self.dram["gc_b2c"][:], t["gcb2c"], sy)
        self.load("sync", t["gcb3"][:], self.dram["gc_b3"][:], t["gcb3"], sy)
        self.load("sync", t["cnb4"][:], self.dram["cn_b4"][:], t["cnb4"], sy)
        self.load("sync", t["ar_sb"][:], self.dram["ar"][:], t["ar_sb"], sy)
        self.seal_group(wsem, [t["cn1_sb"], t["cn2_sb"], t["gc1_sb"],
                               t["cn3_sb"], t["gc2_sb"], t["cn4_sb"],
                               t["gc3_sb"], (t["cnb_rows"], 1),
                               (t["cnb_rows"], 2), (t["cnb_rows"], 3)])
        self.seal_group(sy, [t["gcb1c"], t["gcb2c"],
                             t["gcb3"], t["cnb4"], t["ar_sb"]])

        ctxC1 = ExitStack()
        xn2 = [self.sb(f"c_xn{p}", (P, D), BF16, ctxC1) for p in range(2)]
        gcsum = self.ps("c_gcsum", (P, 512), F32, ctxC1)
        B.op("tensor", lambda e: e.matmul(
            gcsum[:, 0:512], t["zero_row"][:, 0:P], t["zero_row"][:],
            start=True, stop=False), reads=[t["zero_row"]],
            writes=[(gcsum, "open")])
        for i in range(NT):
            par = i % 2
            self.ln_stats(i, t["negm"], t["rstd"])
            self.ln_apply(i, t["negm"], t["rstd"], xn2[par])

            def gcs_mm(e, par=par):
                for c in range(8):
                    inst = e.matmul(gcsum[:, c:c + 1],
                                    xn2[par][:, c * P:(c + 1) * P],
                                    t["onecol_bf"][:],
                                    start=False, stop=False)
                return inst
            B.op("tensor", gcs_mm, reads=[xn2[par], t["onecol_bf"],
                                          (gcsum, "open")], writes=[(gcsum, i)])
        B.op("tensor", lambda e: e.matmul(
            gcsum[:, 0:512], t["zero_row"][:, 0:P], t["zero_row"][:],
            start=False, stop=True), reads=[t["zero_row"]], writes=[gcsum])

        B.op("vector", lambda e: e.tensor_scalar(
            out=t["mean_bf"][:], in0=gcsum[:, 0:8], scalar1=1.0 / NFULL,
            scalar2=None, op0=OP.mult), reads=[gcsum], writes=[t["mean_bf"]])
        h1g = self.ps("c_h1g", (P, 512), F32, ctxC1)
        h2g = self.ps("c_h2g", (P, 512), F32, ctxC1)
        h3g = self.ps("c_h3g", (P, 512), F32, ctxC1)

        def gc1_mm(e):
            for oc in range(4):
                for c in range(8):
                    inst = e.matmul(h1g[:, oc:oc + 1],
                                    t["gc1_sb"][:, c, oc * P:(oc + 1) * P],
                                    t["mean_bf"][:, c:c + 1],
                                    start=(oc == 0 and c == 0),
                                    stop=(oc == 3 and c == 7))
            return inst
        B.op("tensor", gc1_mm, reads=[t["gc1_sb"], t["mean_bf"]], writes=[h1g])

        def gc1_act(e):
            for oc in range(4):
                inst = e.activation(t["h1g_bf"][:, oc:oc + 1], h1g[:, oc:oc + 1],
                                    AF.Gelu, bias=t["gcb1c"][:, oc:oc + 1])
            return inst
        B.op("scalar", gc1_act, reads=[h1g, t["gcb1c"]], writes=[t["h1g_bf"]])

        def gc2_mm(e):
            for oc in range(2):
                for c in range(4):
                    inst = e.matmul(h2g[:, oc:oc + 1],
                                    t["gc2_sb"][:, c, oc * P:(oc + 1) * P],
                                    t["h1g_bf"][:, c:c + 1],
                                    start=(oc == 0 and c == 0),
                                    stop=(oc == 1 and c == 3))
            return inst
        B.op("tensor", gc2_mm, reads=[t["gc2_sb"], t["h1g_bf"]], writes=[h2g])

        def gc2_act(e):
            for oc in range(2):
                inst = e.activation(t["h2g_bf"][:, oc:oc + 1], h2g[:, oc:oc + 1],
                                    AF.Gelu, bias=t["gcb2c"][:, oc:oc + 1])
            return inst
        B.op("scalar", gc2_act, reads=[h2g, t["gcb2c"]], writes=[t["h2g_bf"]])

        def gc3_mm(e):
            for c in range(2):
                inst = e.matmul(h3g[0:1, 0:1], t["gc3_sb"][:, c:c + 1],
                                t["h2g_bf"][:, c:c + 1],
                                start=(c == 0), stop=(c == 1))
            return inst
        B.op("tensor", gc3_mm, reads=[t["gc3_sb"], t["h2g_bf"]], writes=[h3g])
        B.op("scalar", lambda e: e.activation(
            t["gfb"][:], h3g[0:1, 0:1], AF.Sigmoid, bias=t["gcb3"][:]),
            reads=[h3g, t["gcb3"]], writes=[t["gfb"]])
        cst = t["cst"]
        B.op("vector", lambda e: e.tensor_scalar(
            out=cst[:, 4:5], in0=t["gfb"][:], scalar1=1.4, scalar2=None,
            op0=OP.mult), reads=[t["gfb"]], writes=[(cst, 4)])
        B.op("vector", lambda e: e.tensor_tensor(
            out=cst[:, 5:6], in0=cst[:, 4:5], in1=t["ar_sb"][:], op=OP.mult),
            reads=[(cst, 4), t["ar_sb"]], writes=[(cst, 5)])
        B.op("vector", lambda e: e.tensor_scalar(
            out=cst[:, 6:7], in0=cst[:, 5:6], scalar1=0.1, scalar2=None,
            op0=OP.add), reads=[(cst, 5)], writes=[(cst, 6)])
        B.op("vector", lambda e: e.tensor_tensor(
            out=cst[:, 0:1], in0=cst[:, 6:7], in1=t["ar_sb"][:], op=OP.subtract),
            reads=[(cst, 6), t["ar_sb"]], writes=[(cst, 0)])
        B.op("vector", lambda e: e.tensor_scalar(
            out=cst[:, 1:2], in0=t["ar_sb"][:], scalar1=0.6, scalar2=None,
            op0=OP.mult), reads=[t["ar_sb"]], writes=[(cst, 1)])
        B.op("vector", lambda e: e.tensor_scalar(
            out=cst[:, 2:3], in0=cst[:, 0:1], scalar1=-1.0, scalar2=None,
            op0=OP.mult), reads=[(cst, 0)], writes=[(cst, 2)])
        B.op("vector", lambda e: e.tensor_scalar(
            out=cst[:, 3:4], in0=cst[:, 1:2], scalar1=-1.0, scalar2=None,
            op0=OP.mult), reads=[(cst, 1)], writes=[(cst, 3)])
        self.barrier()
        ctxC1.close()

        ctxC2 = ExitStack()
        out_cur = self.sb("c_out0", (P, D), F32, ctxC2)
        out_nxt = self.sb("c_out1", (P, D), F32, ctxC2)
        outT_bf = self.sb("c_outT", (P, 8, P), BF16, ctxC2)
        h1T_bf = self.sb("c_h1T", (P, 8, P), BF16, ctxC2)
        h2T_bf = self.sb("c_h2T", (P, 4, P), BF16, ctxC2)
        h3T_bf = self.sb("c_h3T", (P, 2, P), BF16, ctxC2)
        tf_row = self.sb("c_tf", (1, P), F32, ctxC2)
        cplA = self.sb("c_cplA", (1, P), F32, ctxC2)
        cplB = self.sb("c_cplB", (1, P), F32, ctxC2)
        cplc = self.sb("c_cplc", (P, 2), F32, ctxC2)
        tp2 = self.ps("c_tp2", (P, D), F32, ctxC2)
        h1_ps = self.ps("c_h1", (P, 8, P), F32, ctxC2)
        h2_ps = self.ps("c_h2", (P, 4, P), F32, ctxC2)
        h3_ps = self.ps("c_h3", (P, 512), F32, ctxC2)
        tf_ps = self.ps("c_tfp", (P, 512), F32, ctxC2)
        cplT_ps = self.ps("c_cplT", (P, 512), F32, ctxC2)

        for i in range(NT):
            xa, xkey = self.xtile(i)
            B.op("vector", lambda e, xa=xa, i=i: e.tensor_scalar(
                out=out_cur[:], in0=xa, scalar1=t["negm"][:, i:i + 1],
                scalar2=t["rstd"][:, i:i + 1], op0=OP.add, op1=OP.mult),
                reads=[xkey, (t["negm"], i), (t["rstd"], i)], writes=[out_cur])
            cur, nxt = out_cur, out_nxt
            for k in range(3):
                def tpf(e, cur=cur):
                    for c in range(8):
                        inst = e.transpose(tp2[:, c * P:(c + 1) * P],
                                           cur[:, c * P:(c + 1) * P],
                                           t["identf"][:])
                    return inst
                B.op("tensor", tpf, reads=[cur, t["identf"]], writes=[tp2])
                B.op("vector", lambda e: e.tensor_copy(
                    outT_bf[:, :, :], tp2[:]), reads=[tp2], writes=[outT_bf])

                def h1mm(e):
                    for oc in range(8):
                        e.matmul(h1_ps[:, oc, :],
                                 t["cnb_rows"][:, oc * P:(oc + 1) * P],
                                 t["ones_row"][:],
                                 start=(oc % 4 == 0), stop=False)
                        for c in range(8):
                            inst = e.matmul(
                                h1_ps[:, oc, :],
                                t["cn1_sb"][:, c, oc * P:(oc + 1) * P],
                                outT_bf[:, c, :],
                                start=False, stop=(oc % 4 == 3 and c == 7))
                    return inst
                B.op("tensor", h1mm, reads=[outT_bf, t["cn1_sb"], t["cnb_rows"],
                                            t["ones_row"]], writes=[h1_ps])
                B.op("scalar", lambda e: e.activation(
                    h1T_bf[:, :, :], h1_ps[:, :, :], AF.Gelu),
                    reads=[h1_ps], writes=[h1T_bf])

                def h2mm(e):
                    for oc in range(4):
                        e.matmul(h2_ps[:, oc, :],
                                 t["cnb_rows"][:, D + oc * P:D + (oc + 1) * P],
                                 t["ones_row"][:],
                                 start=(oc == 0), stop=False)
                        for c in range(8):
                            inst = e.matmul(
                                h2_ps[:, oc, :],
                                t["cn2_sb"][:, c, oc * P:(oc + 1) * P],
                                h1T_bf[:, c, :],
                                start=False, stop=(oc == 3 and c == 7))
                    return inst
                B.op("tensor", h2mm, reads=[h1T_bf, t["cn2_sb"], t["cnb_rows"],
                                            t["ones_row"]], writes=[h2_ps])
                B.op("scalar", lambda e: e.activation(
                    h2T_bf[:, :, :], h2_ps[:, :, :], AF.Gelu),
                    reads=[h2_ps], writes=[h2T_bf])

                def h3mm(e):
                    for oc in range(2):
                        e.matmul(h3_ps[:, oc * P:(oc + 1) * P],
                                 t["cnb_rows"][:, D + 512 + oc * P:D + 512 + (oc + 1) * P],
                                 t["ones_row"][:],
                                 start=(oc == 0), stop=False)
                        for c in range(4):
                            inst = e.matmul(
                                h3_ps[:, oc * P:(oc + 1) * P],
                                t["cn3_sb"][:, c, oc * P:(oc + 1) * P],
                                h2T_bf[:, c, :],
                                start=False, stop=(oc == 1 and c == 3))
                    return inst
                B.op("tensor", h3mm, reads=[h2T_bf, t["cn3_sb"], t["cnb_rows"],
                                            t["ones_row"]], writes=[h3_ps])
                B.op("scalar", lambda e: e.activation(
                    h3T_bf[:, :, :], h3_ps[:, 0:256], AF.Gelu),
                    reads=[h3_ps], writes=[h3T_bf])

                def tfmm(e):
                    for c in range(2):
                        inst = e.matmul(tf_ps[0:1, 0:P],
                                        t["cn4_sb"][:, c:c + 1],
                                        h3T_bf[:, c, :],
                                        start=(c == 0), stop=(c == 1))
                    return inst
                B.op("tensor", tfmm, reads=[h3T_bf, t["cn4_sb"]], writes=[tf_ps])
                B.op("scalar", lambda e: e.activation(
                    tf_row[:], tf_ps[0:1, 0:P], AF.Sigmoid, bias=t["cnb4"][:]),
                    reads=[tf_ps, t["cnb4"]], writes=[tf_row])
                B.op("vector", lambda e: e.tensor_scalar(
                    out=cplA[:], in0=tf_row[:], scalar1=cst[0:1, 1:2],
                    scalar2=cst[0:1, 0:1], op0=OP.mult, op1=OP.add),
                    reads=[tf_row, (cst, 0), (cst, 1)], writes=[cplA])
                B.op("vector", lambda e: e.tensor_scalar(
                    out=cplB[:], in0=cplA[:], scalar1=-1.0,
                    scalar2=None, op0=OP.mult),
                    reads=[cplA], writes=[cplB])

                def cpl_tp(e):
                    e.transpose(cplT_ps[0:P, 0:1], cplA[:], t["identf"][0:1, 0:1])
                    return e.transpose(cplT_ps[0:P, 1:2], cplB[:],
                                       t["identf"][0:1, 0:1])
                B.op("tensor", cpl_tp,
                    reads=[cplA, cplB, t["identf"]], writes=[cplT_ps])
                B.op("vector", lambda e: e.tensor_copy(cplc[:], cplT_ps[:, 0:2]),
                    reads=[cplT_ps], writes=[cplc])
                qmap = [(0, 512, 0), (256, 768, 1), (512, 0, 1), (768, 256, 0)]

                def twist(e, cur=cur, nxt=nxt):
                    for (dst, src, neg) in qmap:
                        inst = e.scalar_tensor_tensor(
                            out=nxt[:, dst:dst + 256],
                            in0=cur[:, src:src + 256],
                            scalar=cplc[:, neg:neg + 1],
                            in1=cur[:, dst:dst + 256],
                            op0=OP.mult, op1=OP.add)
                    return inst
                B.op("vector", twist, reads=[cur, cplc], writes=[nxt])
                cur, nxt = nxt, cur
            B.op("vector", lambda e, xa=xa, cur=cur: e.tensor_tensor(
                out=xa, in0=xa, in1=cur[:], op=OP.add),
                reads=[xkey, cur], writes=[xkey])
        self.barrier()
        ctxC2.close()
        ctxC.close()
        if self.cfg.get("tap_l0"):
            self.tap_stream("dbg_l0")

    def phase_D(self):
        B, t = self.B, self.t
        self.B.phase = "D"
        cfg = self.cfg
        NIT = cfg.get("bisect_iters", 36)
        ctxD = ExitStack()
        self.sb("xn3T", (P, 8, NFULL), BF16, ctxD)
        wsem = self.newsem()
        self.sb("wo_sb", (P, 8, 512), BF16, ctxD)
        sy = self.newsem()
        for nm, src in [("bq_c", "bq"), ("bk_c", "bk")]:
            self.sb(nm, (P, 8), F32, ctxD)
            self.load("sync", t[nm][:], self.dram[src][:], t[nm], sy)
        self.sb("bo_bf", (1, D), BF16, ctxD)
        self.sb("bv_bf", (1, D), BF16, ctxD)
        self.load("gpsimd", t["bo_bf"][:], self.dram["bo_row"][:],
                  t["bo_bf"], wsem)
        self.load("gpsimd", t["bv_bf"][:], self.dram["bv_row"][:],
                  t["bv_bf"], wsem)
        self.seal_group(wsem, [t["bo_bf"], t["bv_bf"]])
        self.seal_group(sy, [t["bq_c"], t["bk_c"]])

        ctxD0 = ExitStack()
        self.junk = self.sb("junkD", (P, D), BF16, ctxD0)
        xn3 = [self.sb(f"d_xn{p}", (P, D), BF16, ctxD0) for p in range(2)]
        tp_ps = self.ps("d_tp", (P, D), BF16, ctxD0)
        for i in range(NT):
            par = i % 2
            self.ln_stats(i, t["negm"], t["rstd"])
            self.ln_apply(i, t["negm"], t["rstd"], xn3[par])

            def tpf(e, par=par):
                for c in range(8):
                    inst = e.transpose(tp_ps[:, c * P:(c + 1) * P],
                                       xn3[par][:, c * P:(c + 1) * P],
                                       t["ident"][:])
                return inst
            B.op("tensor", tpf, reads=[xn3[par], t["ident"]], writes=[tp_ps])
            B.op("vector", lambda e, i=i: e.tensor_copy(
                t["xn3T"][:, :, i * P:(i + 1) * P], tp_ps[:].rearrange(
                    "p (c q) -> p c q", q=P)),
                reads=[tp_ps], writes=[(t["xn3T"], i)])
        self.barrier()
        ctxD0.close()
        self.ctx_oth.close()

        half = cfg["half"]
        own_c0 = 16 * half * P
        ctxD1 = ExitStack()
        k2t = self.sb("d_k2t", (P, NFULL), BF16, ctxD1)
        v2 = self.sb("d_v2", (P, 32, P), BF16, ctxD1)
        q2t = self.sb("d_q2t", (P, NOWN // 2), BF16, ctxD1)
        sumv = self.sb("d_sumv", (P, 1), F32, ctxD1)
        s_bfs = [self.sb(f"d_s{p}", (P, NFULL), BF16, ctxD1) for p in range(2)]
        wT = self.sb("d_wT", (P, 32, P), BF16, ctxD1)
        sm = wT.ap().rearrange("p a b -> p (a b)")
        smL = sm[:, 0:2048]
        smR = sm[:, 2048:NFULL]
        wqsl = self.sb("d_wqsl", (P, 8, P), BF16, ctxD1)
        wksl = self.sb("d_wksl", (P, 8, P), BF16, ctxD1)
        wvsl = self.sb("d_wvsl", (P, 8, P), BF16, ctxD1)
        wslsem = [self.newsem(), self.newsem()]
        atnT = self.sb("d_atnT", (P, 8, NOWN // 2), BF16, ctxD1)
        selc = self.sb("d_selc", (P, 48), F32, ctxD1)
        m8 = self.sb("d_m8", (P, 8), F32, ctxD1)
        oh8 = self.sb("d_oh8", (P, 8), F32, ctxD1)
        jk8 = self.sb("d_jk8", (P, 8), F32, ctxD1)
        iota8 = self.sb("d_iota8", (P, 8), F32, ctxD1)
        ninf = self.sb("d_ninf", (P, 1), F32, ctxD1)
        den8 = self.sb("d_den8", (P, 8), F32, ctxD1)
        selm = self.sb("d_selm", (P, 1), mybir.dt.uint8, ctxD1)
        B.op("gpsimd", lambda e: e.memset(t["d_ninf"][:], -1e30),
             writes=[t["d_ninf"]])

        def iota_fill(e):
            for i in range(8):
                inst = e.memset(t["d_iota8"][:, i:i + 1], float(i))
            return inst
        B.op("gpsimd", iota_fill, writes=[t["d_iota8"]])
        rrow = self.sb("d_rrow", (1, P), BF16, ctxD1)
        rb_sb = self.sb("d_rb", (P, P), BF16, ctxD1)
        psP = self.ps("d_psP", (P, 512), F32, ctxD1)
        psQ = self.ps("d_psQ", (P, 512), F32, ctxD1)
        psAV = self.ps("d_psAV", (P, 512), F32, ctxD1)
        psR = self.ps("d_psR", (P, 512), F32, ctxD1)
        psS = self.ps("d_psS", (P, 2048), F32, ctxD1)
        y_ps = psS

        self.B.phase = "D1"
        for mac in range(2):
            mtok0 = mac * (NOWN // 2)
            for hp in range(8):
                wsm = wslsem[hp % 2]
                for nm, src in [("d_wqsl", "Wq"), ("d_wksl", "Wk"),
                                ("d_wvsl", "Wv")]:
                    self.load("gpsimd", t[nm][:, :, :],
                              self.dram[src][hp].rearrange(
                                  "(c p) o -> p c o", p=P), t[nm], wsm)
                self.seal_group(wsm, [t["d_wqsl"], t["d_wksl"], t["d_wvsl"]])
                pp = [psP, psQ]
                for n in range(8):
                    def kproj(e, hp=hp, n=n, ps=pp[n % 2]):
                        for c in range(8):
                            inst = e.matmul(
                                ps[:, 0:512],
                                t["d_wksl"][:, c, :],
                                t["xn3T"][:, c, n * 512:(n + 1) * 512],
                                start=(c == 0), stop=(c == 7))
                        return inst
                    B.op("tensor", kproj, reads=[t["d_wksl"], t["xn3T"]],
                         writes=[pp[n % 2]])
                    B.op("scalar", lambda e, n=n, hp=hp, ps=pp[n % 2]: e.activation(
                        k2t[:, n * 512:(n + 1) * 512], ps[:, 0:512],
                        AF.Identity, bias=t["bk_c"][:, hp:hp + 1]),
                        reads=[pp[n % 2], t["bk_c"]], writes=[(k2t, n)])
                for n in range(32):
                    def vproj(e, hp=hp, n=n, ps=pp[n % 2]):
                        e.matmul(ps[:, 0:P], t["ones_row"][:],
                                 t["bv_bf"][:, hp * P:(hp + 1) * P],
                                 start=True, stop=False)
                        for c in range(8):
                            inst = e.matmul(
                                ps[:, 0:P],
                                t["xn3T"][:, c, n * P:(n + 1) * P],
                                t["d_wvsl"][:, c, :],
                                start=False, stop=(c == 7))
                        return inst
                    B.op("tensor", vproj, reads=[t["d_wvsl"], t["xn3T"],
                                                 t["ones_row"], t["bv_bf"]],
                         writes=[pp[n % 2]])
                    B.op("vector", lambda e, n=n, ps=pp[n % 2]: e.tensor_copy(
                        v2[:, n, :], ps[:, 0:P]),
                        reads=[pp[n % 2]], writes=[(v2, n)])
                for n in range(2):
                    def qproj(e, hp=hp, n=n, mtok0=mtok0, ps=pp[n % 2]):
                        for c in range(8):
                            inst = e.matmul(
                                ps[:, 0:512],
                                t["d_wqsl"][:, c, :],
                                t["xn3T"][:, c, own_c0 + mtok0 + n * 512:
                                          own_c0 + mtok0 + (n + 1) * 512],
                                start=(c == 0), stop=(c == 7))
                        return inst
                    B.op("tensor", qproj, reads=[t["d_wqsl"], t["xn3T"]],
                         writes=[pp[n % 2]])
                    B.op("scalar", lambda e, n=n, hp=hp, ps=pp[n % 2]: e.activation(
                        q2t[:, n * 512:(n + 1) * 512], ps[:, 0:512],
                        AF.Identity, bias=t["bq_c"][:, hp:hp + 1]),
                        reads=[pp[n % 2], t["bq_c"]], writes=[(q2t, n)])
                for tl in range(8):
                    for hh in range(2):
                        for sh in range(2):
                            def smm(e, tl=tl, hh=hh, sh=sh):
                                for n in range(4):
                                    inst = e.matmul(
                                        psS[:, n * 512:(n + 1) * 512],
                                        q2t[64 * hh:64 * hh + 64,
                                            tl * P:(tl + 1) * P],
                                        k2t[64 * hh:64 * hh + 64,
                                            sh * 2048 + n * 512:
                                            sh * 2048 + (n + 1) * 512],
                                        start=True, stop=True,
                                        tile_position=(64 * hh, 0))
                                return inst
                            B.op("tensor", smm, reads=[(q2t, 0), (q2t, 1),
                                                       (k2t, None)],
                                 writes=[psS])
                            B.op("scalar", lambda e, hh=hh, sh=sh: e.activation(
                                s_bfs[hh][:, sh * 2048:(sh + 1) * 2048],
                                psS[:, 0:2048], AF.Identity, scale=0.125,
                                accum_out=selc[:, 24 * hh + 20 + sh:
                                               24 * hh + 21 + sh]),
                                reads=[psS], writes=[(s_bfs[hh], sh),
                                                     (selc, 24 * hh + 20 + sh)])
                    tapsel = (self.cfg.get("tap_sel") is not None
                              and (mac, hp, tl) == tuple(
                                  self.cfg["tap_sel"]))
                    if tapsel:
                        tsem = self.newsem()
                        for nm, src in (("dbg_raw0", s_bfs[0]),
                                        ("dbg_raw1", s_bfs[1])):
                            B.op("gpsimd", lambda e, nm=nm, src=src: e.dma_start(
                                out=self.dram[nm][:, :], in_=src[:]),
                                reads=[(src, 0), (src, 1)], sem=tsem, inc=16)
                        self.final_waits = getattr(self, "final_waits", [])
                        self.final_waits += [(tsem, 32)]
                    self.select_pair(tl, s_bfs, selc, sm, smL, smR, wT, m8,
                                     oh8, jk8, iota8, den8)
                    if tapsel:
                        tsem2 = self.newsem()
                        for nm, src, rds in (
                                ("dbg_selc", selc, [(selc, None)]),
                                ("dbg_den8", den8, [(den8, None)]),
                                ("dbg_w0", s_bfs[0], [(s_bfs[0], 0),
                                                      (s_bfs[0], 1)]),
                                ("dbg_w1", s_bfs[1], [(s_bfs[1], 0),
                                                      (s_bfs[1], 1)])):
                            B.op("gpsimd", lambda e, nm=nm, src=src: e.dma_start(
                                out=self.dram[nm][:, :], in_=src[:]),
                                reads=rds, sem=tsem2, inc=16)
                        self.final_waits += [(tsem2, 64)]
                    for hh in range(2):
                        u = tl * 2 + hh
                        s_bf = s_bfs[hh]
                        dsem = self.newsem() if not hasattr(self, "_dsem") else self._dsem
                        self._dsem = dsem
                        B.op("sync", lambda e, s_bf=s_bf: e.dma_start_transpose(
                            wT[:, :, :], s_bf[:]),
                            reads=[(s_bf, 0), (s_bf, 1)], writes=[wT],
                            sem=dsem, inc=16)

                        def avmm(e, hh=hh):
                            for c in range(32):
                                inst = e.matmul(
                                    psAV[64 * hh:64 * hh + 64, 0:P],
                                    v2[:, c, 64 * hh:64 * hh + 64],
                                    wT[:, c, :],
                                    start=(c == 0), stop=(c == 31),
                                    tile_position=(0, 64 * hh))
                            return inst
                        B.op("tensor", avmm, reads=[v2, wT], writes=[psAV])
                        B.op("tensor", lambda e, u8=u % 8: e.transpose(
                            psR[0:1, 0:P], den8[:, u8:u8 + 1],
                            t["identf"][:]),
                            reads=[(den8, u % 8), t["identf"]], writes=[psR])
                        def rmove(e):
                            with self.nc.allow_low_precision(
                                    reason="den row to bf16"):
                                return e.tensor_scalar(
                                    out=rrow[:], in0=psR[0:1, 0:P],
                                    scalar1=0.0, scalar2=None, op0=OP.add)
                        B.op("vector", rmove, reads=[psR], writes=[rrow])

                        def rrec(e):
                            with self.nc.allow_low_precision(
                                    reason="recip denom row in bf16"):
                                return e.reciprocal(rrow[:], rrow[:])
                        B.op("vector", rrec, reads=[rrow], writes=[rrow])
                        B.op("tensor", lambda e, hh=hh: e.matmul(
                            psR[64 * hh:64 * hh + 64, 0:P],
                            t["ones_row"][:, 0:64], rrow[:],
                            start=True, stop=True, tile_position=(0, 64 * hh)),
                            reads=[rrow, t["ones_row"]], writes=[psR])
                        B.op("vector", lambda e, hh=hh: e.tensor_copy(
                            rb_sb[64 * hh:64 * hh + 64, :],
                            psR[64 * hh:64 * hh + 64, 0:P]), reads=[psR],
                            writes=[rb_sb])
                        B.op("vector", lambda e, hh=hh, hp=hp, tl=tl: e.tensor_tensor(
                            out=atnT[64 * hh:64 * hh + 64, hp,
                                     tl * P:(tl + 1) * P],
                            in0=psAV[64 * hh:64 * hh + 64, 0:P],
                            in1=rb_sb[64 * hh:64 * hh + 64, :],
                            op=OP.mult),
                            reads=[psAV, rb_sb],
                            writes=[(atnT, (hp, tl, hh))])
            if self.cfg.get("tap_atn") == mac:
                asem = self.newsem()
                B.op("gpsimd", lambda e: e.dma_start(
                    out=self.dram["dbg_atnT"][:, :, :], in_=atnT[:]),
                    reads=[(atnT, None)], sem=asem, inc=16)
                self.final_waits = getattr(self, "final_waits", [])
                self.final_waits += [(asem, 16)]
            for oc in range(2):
                self.load("gpsimd", t["wo_sb"][:, :, :],
                          self.dram["Wo"][:, oc * 512:(oc + 1) * 512].rearrange(
                              "(c p) o -> p c o", p=P),
                          t["wo_sb"], wslsem[oc % 2])
                self.seal_group(wslsem[oc % 2], [t["wo_sb"]])
                for tl in range(8):
                    def ymm(e, tl=tl, oc=oc):
                        e.matmul(y_ps[:, tl % 4 * 512:(tl % 4 + 1) * 512],
                                 t["ones_row"][:],
                                 t["bo_bf"][:, oc * 512:(oc + 1) * 512],
                                 start=True, stop=False)
                        for c in range(8):
                            inst = e.matmul(
                                y_ps[:, tl % 4 * 512:(tl % 4 + 1) * 512],
                                atnT[:, c, tl * P:(tl + 1) * P],
                                t["wo_sb"][:, c, :],
                                start=False, stop=(c == 7))
                        return inst
                    B.op("tensor", ymm, reads=[atnT, t["wo_sb"], t["bo_bf"],
                                               t["ones_row"]],
                         writes=[(psS, tl % 4)])
                    ti = 16 * half + mac * 8 + tl
                    xa, xkey = self.xtile(ti)
                    B.op("vector", lambda e, xa=xa, tl=tl, oc=oc: e.tensor_tensor(
                        out=xa[:, oc * 512:(oc + 1) * 512], in0=xa[:, oc * 512:(oc + 1) * 512],
                        in1=y_ps[:, tl % 4 * 512:(tl % 4 + 1) * 512], op=OP.add),
                        reads=[xkey, (psS, tl % 4)], writes=[xkey])
        self.barrier()
        ctxD1.close()
        ctxD.close()
        if self.cfg.get("tap_l1a"):
            rr = self.dram["dbg_l1a"].rearrange("(i p) d -> p i d", p=P)
            ks = self.newsem()
            B.op("sync", lambda e: e.dma_start(
                out=rr[:, :, :], in_=t["x_own"][:, :, :]),
                reads=[t["x_own"]], sem=ks, inc=16)
            self.final_waits = getattr(self, "final_waits", [])
            self.final_waits += [(ks, 16)]

    def select_pair(self, tl, s_bfs, selc, sm, smL, smR, wT, m8, oh8, jk8,
                    iota8, den8, NB=13):
        B, t = self.B, self.t
        U = (0, 1)

        def C(u, i):
            return selc[:, 24 * u + i:24 * u + i + 1]

        def V(fn, reads, writes):
            B.op("vector", fn, reads=reads, writes=writes)

        def ck(u, key):
            return (selc, 24 * u + key)

        def count(u, thr_i, out_i):
            V(lambda e, u=u: e.tensor_scalar(
                out=smL, in0=s_bfs[u][:, 0:2048], scalar1=C(u, thr_i),
                scalar2=None, op0=OP.is_gt, op1=OP.add,
                accum_out=C(u, 10)),
              [(s_bfs[u], 0), ck(u, thr_i)], [(wT, 0), ck(u, 10)])
            B.op("scalar", lambda e, u=u: e.activation(
                smR, s_bfs[u][:, 2048:NFULL], AF.Sign, bias=C(u, thr_i),
                scale=-1.0, accum_out=C(u, 19)),
                reads=[(s_bfs[u], 1), ck(u, thr_i)],
                writes=[(wT, 1), ck(u, 19)])
            V(lambda e, u=u: e.scalar_tensor_tensor(
                out=C(u, out_i), in0=C(u, 19), scalar=-0.5, op0=OP.mult,
                in1=C(u, 10), op1=OP.add),
              [ck(u, 19), ck(u, 10)], [ck(u, out_i)])

        TH = 408.5 - 1024.0
        for u in U:
            B.op("scalar", lambda e, u=u: e.activation(
                sm, s_bfs[u][:], AF.Square, accum_out=C(u, 22)),
                reads=[(s_bfs[u], 0), (s_bfs[u], 1)],
                writes=[(wT, 0), (wT, 1), ck(u, 22)])
        for u in U:
            V(lambda e, u=u: e.tensor_tensor(out=C(u, 15), in0=C(u, 20),
                                             in1=C(u, 21), op=OP.add),
              [ck(u, 20), ck(u, 21)], [ck(u, 15)])
            V(lambda e, u=u: e.tensor_scalar(out=C(u, 0), in0=C(u, 15),
                                             scalar1=1.0 / NFULL,
                                             scalar2=None, op0=OP.mult),
              [ck(u, 15)], [ck(u, 0)])
            V(lambda e, u=u: e.tensor_scalar(out=C(u, 1), in0=C(u, 22),
                                             scalar1=1.0 / NFULL,
                                             scalar2=None, op0=OP.mult),
              [ck(u, 22)], [ck(u, 1)])
            V(lambda e, u=u: e.tensor_tensor(out=C(u, 16), in0=C(u, 0),
                                             in1=C(u, 0), op=OP.mult),
              [ck(u, 0)], [ck(u, 16)])
            V(lambda e, u=u: e.tensor_tensor(out=C(u, 2), in0=C(u, 1),
                                             in1=C(u, 16), op=OP.subtract),
              [ck(u, 1), ck(u, 16)], [ck(u, 2)])
        for u in U:
            B.op("scalar", lambda e, u=u: e.activation(
                C(u, 3), C(u, 2), AF.Sqrt, bias=t["epscol"][:, 0:1]),
                reads=[ck(u, 2), t["epscol"]], writes=[ck(u, 3)])
        for u in U:
            for col, zz in ((4, 1.215), (5, 1.358), (17, -4.0), (18, 4.0)):
                V(lambda e, u=u, col=col, zz=zz: e.scalar_tensor_tensor(
                    out=C(u, col), in0=C(u, 3), scalar=zz, in1=C(u, 0),
                    op0=OP.mult, op1=OP.add),
                  [ck(u, 3), ck(u, 0)], [ck(u, col)])
        for u in U:
            count(u, 4, 10)
        for u in U:
            V(lambda e, u=u: e.tensor_scalar(
                out=t["d_selm"][:], in0=C(u, 10), scalar1=TH, scalar2=None,
                op0=OP.is_le), [ck(u, 10)], [t["d_selm"]])
            V(lambda e, u=u: e.copy_predicated(C(u, 4), t["d_selm"][:],
                                               C(u, 17)),
              [t["d_selm"], ck(u, 17)], [ck(u, 4)])
        for u in U:
            count(u, 5, 10)
        for u in U:
            V(lambda e, u=u: e.tensor_scalar(
                out=t["d_selm"][:], in0=C(u, 10), scalar1=TH, scalar2=None,
                op0=OP.is_gt), [ck(u, 10)], [t["d_selm"]])
            V(lambda e, u=u: e.copy_predicated(C(u, 5), t["d_selm"][:],
                                               C(u, 18)),
              [t["d_selm"], ck(u, 18)], [ck(u, 5)])
        for u in U:
            V(lambda e, u=u: e.tensor_tensor(out=C(u, 15), in0=C(u, 4),
                                             in1=C(u, 5), op=OP.add),
              [ck(u, 4), ck(u, 5)], [ck(u, 15)])
            V(lambda e, u=u: e.tensor_scalar(out=C(u, 6), in0=C(u, 15),
                                             scalar1=0.5, scalar2=None,
                                             op0=OP.mult),
              [ck(u, 15)], [ck(u, 6)])
            V(lambda e, u=u: e.tensor_tensor(out=C(u, 16), in0=C(u, 5),
                                             in1=C(u, 4), op=OP.subtract),
              [ck(u, 4), ck(u, 5)], [ck(u, 16)])
            V(lambda e, u=u: e.tensor_scalar(out=C(u, 7), in0=C(u, 16),
                                             scalar1=0.25, scalar2=None,
                                             op0=OP.mult),
              [ck(u, 16)], [ck(u, 7)])
            V(lambda e, u=u: e.tensor_scalar(out=C(u, 8), in0=C(u, 16),
                                             scalar1=0.5, scalar2=None,
                                             op0=OP.mult),
              [ck(u, 16)], [ck(u, 8)])
        for k in range(NB):
            for u in U:
                count(u, 6, 10)
            for u in U:
                V(lambda e, u=u: e.tensor_scalar(
                    out=C(u, 9), in0=C(u, 10), scalar1=TH, scalar2=C(u, 8),
                    op0=OP.is_gt, op1=OP.mult),
                  [ck(u, 10), ck(u, 8)], [ck(u, 9)])
                V(lambda e, u=u: e.scalar_tensor_tensor(
                    out=C(u, 15), in0=C(u, 9), scalar=C(u, 7),
                    op0=OP.subtract, in1=C(u, 6), op1=OP.add),
                  [ck(u, 9), ck(u, 7), ck(u, 6)], [ck(u, 15)])
                V(lambda e, u=u: e.tensor_copy(C(u, 6), C(u, 15)),
                  [ck(u, 15)], [ck(u, 6)])
                V(lambda e, u=u: e.tensor_scalar(
                    out=selc[:, 24 * u + 7:24 * u + 9],
                    in0=selc[:, 24 * u + 7:24 * u + 9], scalar1=0.5,
                    scalar2=None, op0=OP.mult),
                  [ck(u, 7), ck(u, 8)], [ck(u, 7), ck(u, 8)])
        for u in U:
            V(lambda e, u=u: e.tensor_tensor(out=C(u, 4), in0=C(u, 6),
                                             in1=C(u, 8), op=OP.subtract),
              [ck(u, 6), ck(u, 8)], [ck(u, 4)])
        for u in U:
            count(u, 4, 11)
        for u in U:
            V(lambda e, u=u: e.tensor_scalar(
                out=C(u, 12), in0=C(u, 11), scalar1=1024.0 - 409.0,
                scalar2=None, op0=OP.add),
              [ck(u, 11)], [ck(u, 12)])
            V(lambda e, u=u: e.tensor_scalar(
                out=C(u, 12), in0=C(u, 12), scalar1=7.0, scalar2=0.0,
                op0=OP.min, op1=OP.max),
              [ck(u, 12)], [ck(u, 12)])
        for u in U:
            V(lambda e, u=u: e.scalar_tensor_tensor(
                out=sm, in0=s_bfs[u][:], scalar=C(u, 4), op0=OP.is_le,
                in1=t["d_ninf"][:, 0:1].to_broadcast([P, NFULL]),
                op1=OP.mult),
              [(s_bfs[u], 0), (s_bfs[u], 1), ck(u, 4), t["d_ninf"]],
              [(wT, 0), (wT, 1)])
            V(lambda e, u=u: e.scalar_tensor_tensor(
                out=sm, in0=s_bfs[u][:], scalar=-1.0, op0=OP.mult,
                in1=sm, op1=OP.add),
              [(s_bfs[u], 0), (s_bfs[u], 1), (wT, 0), (wT, 1)],
              [(wT, 0), (wT, 1)])
            V(lambda e: e.max(out=m8[:], in_=sm), [(wT, 0), (wT, 1)], [m8])
            V(lambda e, u=u: e.tensor_scalar(out=oh8[:], in0=iota8[:],
                                             scalar1=C(u, 12), scalar2=None,
                                             op0=OP.is_equal),
              [t["d_iota8"], ck(u, 12)], [oh8])
            V(lambda e: e.tensor_tensor(out=oh8[:], in0=oh8[:], in1=m8[:],
                                        op=OP.mult), [oh8, m8], [oh8])
            V(lambda e, u=u: e.tensor_scalar(out=jk8[:], in0=oh8[:],
                                             scalar1=0.0, scalar2=None,
                                             op0=OP.add, op1=OP.add,
                                             accum_out=C(u, 13)),
              [oh8], [jk8, ck(u, 13)])
            V(lambda e, u=u: e.tensor_scalar(out=C(u, 14), in0=C(u, 13),
                                             scalar1=-1.0, scalar2=None,
                                             op0=OP.mult),
              [ck(u, 13)], [ck(u, 14)])
            V(lambda e, u=u: e.scalar_tensor_tensor(
                out=sm, in0=s_bfs[u][:], scalar=C(u, 14), in1=s_bfs[u][:],
                op0=OP.is_ge, op1=OP.mult),
              [(s_bfs[u], 0), (s_bfs[u], 1), ck(u, 14)], [(wT, 0), (wT, 1)])
            B.op("scalar", lambda e, u=u, u8=(tl * 2 + u) % 8: e.activation(
                s_bfs[u][:], sm, AF.Exp, accum_out=den8[:, u8:u8 + 1]),
                reads=[(wT, 0), (wT, 1)],
                writes=[(s_bfs[u], 0), (s_bfs[u], 1), (den8, (tl * 2 + u) % 8)])

    def bisect_unit(self, u8, NIT, s_bf, lo8, hi8, mid8, mtmp, cnt8, msk8,
                    mskn8, sm, smkey):
        B, t = self.B, self.t
        B.op("vector", lambda e, u8=u8: e.memset(mid8[:, u8:u8 + 1], 0.0),
             writes=[(mid8, u8)])
        for k in range(NIT):
            step = 64.0 * (2.0 ** -(k + 1))
            B.op("vector", lambda e, u8=u8: e.tensor_scalar(
                out=sm, in0=s_bf[:], scalar1=mid8[:, u8:u8 + 1],
                scalar2=None, op0=OP.is_gt, op1=OP.add,
                accum_out=cnt8[:, u8:u8 + 1]),
                reads=[(s_bf, 0), (s_bf, 1), (mid8, u8)],
                writes=[smkey, (cnt8, u8)])
            B.op("vector", lambda e, u8=u8, st=step: e.tensor_scalar(
                out=mtmp[:, u8:u8 + 1], in0=cnt8[:, u8:u8 + 1], scalar1=408.5,
                scalar2=2.0 * st, op0=OP.is_gt, op1=OP.mult),
                reads=[(cnt8, u8)], writes=[(mtmp, u8)])
            B.op("vector", lambda e, u8=u8, st=step: e.scalar_tensor_tensor(
                out=mid8[:, u8:u8 + 1], in0=mtmp[:, u8:u8 + 1], scalar=-st,
                in1=mid8[:, u8:u8 + 1], op0=OP.add, op1=OP.add),
                reads=[(mtmp, u8), (mid8, u8)], writes=[(mid8, u8)])
        B.op("vector", lambda e, u8=u8, st=64.0 * (2.0 ** -NIT): e.tensor_scalar(
            out=lo8[:, u8:u8 + 1], in0=mid8[:, u8:u8 + 1], scalar1=-st,
            scalar2=None, op0=OP.add),
            reads=[(mid8, u8)], writes=[(lo8, u8)])

    def phase_E(self):
        B, t = self.B, self.t
        self.B.phase = "E"
        half = self.cfg["half"]
        ctxE = ExitStack()
        self.sb("tT", (P, 8, NOWN), BF16, ctxE)
        self.sb("y_acc", (P, 16, D), BF16, ctxE)
        self.sb("gate_sb", (P, 8, 8), BF16, ctxE)
        self.sb("gateb", (8, 1), F32, ctxE)
        self.sb("ns_row", (1, D), BF16, ctxE)
        self.sb("ns_bc", (P, D), BF16, ctxE)
        self.sb("b2all", (8, D), BF16, ctxE)
        self.sb("W8", (P, 16, 8), F32, ctxE)
        self.sb("weT", (8, 16, P), BF16, ctxE)
        self.sb("rms3", (P, 16), F32, ctxE)
        self.sb("srt8", (P, 8), F32, ctxE)
        self.sb("lg_f", (P, 8), F32, ctxE)
        self.sb("lgT_f", (8, P), F32, ctxE)
        self.sb("p1c", (P, 1), F32, ctxE)
        self.sb("p2c", (P, 1), F32, ctxE)
        self.sb("eqa", (P, 8), F32, ctxE)
        self.sb("eqb", (P, 8), F32, ctxE)
        self.sb("d12", (P, 1), F32, ctxE)
        wsem = self.newsem()
        self.load("gpsimd", t["gate_sb"][:, :, :],
                  self.dram["gate_w"].rearrange("(c p) o -> p c o", p=P),
                  t["gate_sb"], wsem)
        self.load("gpsimd", t["ns_row"][:], self.dram["norm_scale"][:],
                  t["ns_row"], wsem)
        self.load("gpsimd", t["b2all"][:, :], self.dram["mlp2_b"][:, :],
                  t["b2all"], wsem)
        sy2 = self.newsem()
        self.load("sync", t["gateb"][:], self.dram["gate_b"][:], t["gateb"], sy2)
        self.seal_group(wsem, [t["gate_sb"], t["ns_row"], t["b2all"]])

        ctxE0 = ExitStack()
        self.junk = self.sb("junkE", (P, D), BF16, ctxE0)
        tbf = [self.sb(f"e_t{p}", (P, D), BF16, ctxE0) for p in range(2)]
        tp_ps = self.ps("e_tp", (P, D), BF16, ctxE0)
        lg_ps = self.ps("e_lg", (P, 512), F32, ctxE0)
        tg_ps = self.ps("e_tg", (P, 512), F32, ctxE0)
        nb_ps = self.ps("e_nb", (P, 512), F32, ctxE0)

        B.op("tensor", lambda e: e.matmul(
            nb_ps[:, 0:512], t["ones_row"][:], t["ns_row"][:, 0:512],
            start=True, stop=True), reads=[t["ones_row"], t["ns_row"]],
            writes=[nb_ps])
        B.op("vector", lambda e: e.tensor_copy(t["ns_bc"][:, 0:512],
                                               nb_ps[:, 0:512]),
             reads=[nb_ps], writes=[(t["ns_bc"], 0)])
        B.op("tensor", lambda e: e.matmul(
            nb_ps[:, 0:512], t["ones_row"][:], t["ns_row"][:, 512:1024],
            start=True, stop=True), reads=[t["ones_row"], t["ns_row"]],
            writes=[nb_ps])
        B.op("vector", lambda e: e.tensor_copy(t["ns_bc"][:, 512:1024],
                                               nb_ps[:, 0:512]),
             reads=[nb_ps], writes=[(t["ns_bc"], 1)])

        st = t["stat"]
        ubf = [self.sb(f"e_u{p}", (P, D), BF16, ctxE0) for p in range(2)]
        for j in range(16):
            ti = 16 * half + j
            xa, xkey = self.xtile(ti)
            par = j % 2
            c0 = 8 * par
            self.ln_stats(ti, t["negm"], t["rstd"])
            self.ln_apply(ti, t["negm"], t["rstd"], ubf[par])
            B.op("scalar", lambda e, par=par, c0=c0, jk=self.junk: e.activation(
                jk[:], ubf[par][:], AF.Square,
                accum_out=st[:, c0:c0 + 1]),
                reads=[ubf[par]], writes=[(st, c0), self.junk])
            B.op("vector", lambda e, c0=c0: e.tensor_scalar(
                out=st[:, c0 + 1:c0 + 2], in0=st[:, c0:c0 + 1],
                scalar1=1.0 / D, scalar2=None, op0=OP.mult),
                reads=[(st, c0)], writes=[(st, c0 + 1)])
            B.op("scalar", lambda e, c0=c0: e.activation(
                st[:, c0 + 2:c0 + 3], st[:, c0 + 1:c0 + 2], AF.Sqrt,
                bias=t["epscol"][:, 0:1]),
                reads=[(st, c0 + 1), t["epscol"]], writes=[(st, c0 + 2)])
            B.op("vector", lambda e, c0=c0, j=j: e.reciprocal(
                t["rms3"][:, j:j + 1], st[:, c0 + 2:c0 + 3]),
                reads=[(st, c0 + 2)], writes=[(t["rms3"], j)])
            B.op("vector", lambda e, j=j, par=par: e.scalar_tensor_tensor(
                out=tbf[par][:], in0=ubf[par][:], scalar=t["rms3"][:, j:j + 1],
                in1=t["ns_bc"][:], op0=OP.mult, op1=OP.mult),
                reads=[ubf[par], (t["rms3"], j), (t["ns_bc"], 0),
                       (t["ns_bc"], 1)], writes=[tbf[par]])

            def tpf(e, par=par):
                for c in range(8):
                    inst = e.transpose(tp_ps[:, c * P:(c + 1) * P],
                                       tbf[par][:, c * P:(c + 1) * P],
                                       t["ident"][:])
                return inst
            B.op("tensor", tpf, reads=[tbf[par], t["ident"]], writes=[tp_ps])
            B.op("vector", lambda e, j=j: e.tensor_copy(
                t["tT"][:, :, j * P:(j + 1) * P],
                tp_ps[:].rearrange("p (c q) -> p c q", q=P)),
                reads=[tp_ps], writes=[(t["tT"], j)])

            def lgmm(e, j=j):
                for c in range(8):
                    inst = e.matmul(lg_ps[0:8, 0:P], t["gate_sb"][:, c, :],
                                    t["tT"][:, c, j * P:(j + 1) * P],
                                    start=(c == 0), stop=(c == 7))
                return inst
            B.op("tensor", lgmm, reads=[(t["tT"], j), t["gate_sb"]],
                 writes=[lg_ps])
            B.op("scalar", lambda e: e.activation(
                t["lgT_f"][:], lg_ps[0:8, 0:P], AF.Identity,
                bias=t["gateb"][:]),
                reads=[lg_ps, t["gateb"]], writes=[t["lgT_f"]])
            B.op("tensor", lambda e: e.transpose(
                tg_ps[:, 0:8], t["lgT_f"][:], t["identf"][0:8, 0:8]),
                reads=[t["lgT_f"], t["identf"]], writes=[tg_ps])
            B.op("vector", lambda e: e.tensor_copy(t["lg_f"][:], tg_ps[:, 0:8]),
                 reads=[tg_ps], writes=[t["lg_f"]])
            B.op("vector", lambda e: e.max(out=t["srt8"][:], in_=t["lg_f"][:]),
                 reads=[t["lg_f"]], writes=[t["srt8"]])
            B.op("vector", lambda e: e.tensor_tensor(
                out=t["d12"][:], in0=t["srt8"][:, 0:1], in1=t["srt8"][:, 1:2],
                op=OP.subtract), reads=[t["srt8"]], writes=[t["d12"]])
            B.op("scalar", lambda e: e.activation(
                t["p1c"][:], t["d12"][:], AF.Sigmoid),
                reads=[t["d12"]], writes=[t["p1c"]])
            B.op("vector", lambda e: e.tensor_scalar(
                out=t["p2c"][:], in0=t["p1c"][:], scalar1=-1.0, scalar2=1.0,
                op0=OP.mult, op1=OP.add), reads=[t["p1c"]], writes=[t["p2c"]])
            B.op("vector", lambda e: e.tensor_scalar(
                out=t["eqa"][:], in0=t["lg_f"][:], scalar1=t["srt8"][:, 0:1],
                scalar2=None, op0=OP.is_equal),
                reads=[t["lg_f"], t["srt8"]], writes=[t["eqa"]])
            B.op("vector", lambda e: e.tensor_scalar(
                out=t["eqa"][:], in0=t["eqa"][:], scalar1=t["p1c"][:, 0:1],
                scalar2=None, op0=OP.mult),
                reads=[t["eqa"], t["p1c"]], writes=[t["eqa"]])
            B.op("vector", lambda e: e.tensor_scalar(
                out=t["eqb"][:], in0=t["lg_f"][:], scalar1=t["srt8"][:, 1:2],
                scalar2=None, op0=OP.is_equal),
                reads=[t["lg_f"], t["srt8"]], writes=[t["eqb"]])
            B.op("vector", lambda e: e.tensor_scalar(
                out=t["eqb"][:], in0=t["eqb"][:], scalar1=t["p2c"][:, 0:1],
                scalar2=None, op0=OP.mult),
                reads=[t["eqb"], t["p2c"]], writes=[t["eqb"]])
            B.op("vector", lambda e, j=j: e.tensor_tensor(
                out=t["W8"][:, j, :], in0=t["eqa"][:], in1=t["eqb"][:],
                op=OP.add), reads=[t["eqa"], t["eqb"]], writes=[(t["W8"], j)])
            B.op("tensor", lambda e, j=j: e.transpose(
                lg_ps[0:8, 128:256], t["W8"][:, j, :], t["identf"][:]),
                reads=[(t["W8"], j), t["identf"]], writes=[(lg_ps, 1)])
            B.op("vector", lambda e, j=j: e.tensor_copy(
                t["weT"][:, j, :], lg_ps[0:8, 128:256]),
                reads=[(lg_ps, 1)], writes=[(t["weT"], j)])
        self.barrier()
        ctxE0.close()

        ctxE1 = ExitStack()
        w1b = [self.sb(f"e_w1{p}", (P, 8, D), BF16, ctxE1) for p in range(2)]
        w2b = [self.sb(f"e_w2{p}", (P, 4, D), BF16, ctxE1) for p in range(2)]
        b1r = [self.sb(f"e_b1{p}", (1, D), BF16, ctxE1) for p in range(2)]
        xg = self.sb("e_xg", (P, 512), BF16, ctxE1)
        xl = self.sb("e_xl", (P, 512), BF16, ctxE1)
        sg = self.sb("e_sg", (P, 512), BF16, ctxE1)
        t1b = self.sb("e_tb1", (P, 512), BF16, ctxE1)
        swb = self.sb("e_sw", (P, 512), BF16, ctxE1)
        swT = self.sb("e_swT", (P, 4, P), BF16, ctxE1)
        psH = self.ps("e_h", (P, D), F32, ctxE1)
        psH2 = self.ps("e_h2", (P, D), F32, ctxE1)
        psY = self.ps("e_y", (P, D), F32, ctxE1)
        tpE = self.ps("e_tp1", (P, 512), BF16, ctxE1)
        wsems = [self.newsem(), self.newsem()]
        m1 = self.dram["mlp1_w"]
        m2 = self.dram["mlp2_w"]
        m1b = self.dram["mlp1_b"]
        hps = [psH, psH2]
        for ei in range(8):
            for q in range(4):
                par = (ei * 4 + q) % 2
                ws = wsems[par]
                self.load("gpsimd", w1b[par][:, :, :],
                          m1[ei, :, q * D:(q + 1) * D].rearrange(
                              "(c p) o -> p c o", p=P), w1b[par], ws)
                self.load("gpsimd", w2b[par][:, :, :],
                          m2[ei, q * 512:(q + 1) * 512, :].rearrange(
                              "(c p) o -> p c o", p=P), w2b[par], ws)
                self.load("gpsimd", b1r[par][:],
                          m1b[ei:ei + 1, q * D:(q + 1) * D], b1r[par], ws)
                self.seal_group(ws, [w1b[par], w2b[par], b1r[par]])
                for j in range(16):
                    hp = hps[j % 2]

                    def hmm(e, par=par, j=j, hp=hp):
                        for oc in range(2):
                            e.matmul(hp[:, oc * 512:(oc + 1) * 512],
                                     t["ones_row"][:],
                                     b1r[par][:, oc * 512:(oc + 1) * 512],
                                     start=True, stop=False)
                            for c in range(8):
                                inst = e.matmul(
                                    hp[:, oc * 512:(oc + 1) * 512],
                                    t["tT"][:, c, j * P:(j + 1) * P],
                                    w1b[par][:, c, oc * 512:(oc + 1) * 512],
                                    start=False, stop=(c == 7))
                        return inst
                    B.op("tensor", hmm, reads=[(t["tT"], j), w1b[par],
                                               b1r[par], t["ones_row"]],
                         writes=[hp])
                    B.op("vector", lambda e, hp=hp: e.tensor_scalar(
                        out=xg[:], in0=hp[:, 0:D:2], scalar1=7.0, scalar2=None,
                        op0=OP.min), reads=[hp], writes=[xg])
                    B.op("vector", lambda e, hp=hp: e.tensor_scalar(
                        out=xl[:], in0=hp[:, 1:D:2], scalar1=7.0, scalar2=-7.0,
                        op0=OP.min, op1=OP.max), reads=[hp], writes=[xl])
                    B.op("scalar", lambda e: e.activation(
                        sg[:], xg[:], AF.Sigmoid, scale=1.702),
                        reads=[xg], writes=[sg])
                    B.op("vector", lambda e: e.scalar_tensor_tensor(
                        out=t1b[:], in0=xl[:], scalar=1.0, in1=sg[:],
                        op0=OP.add, op1=OP.mult), reads=[xl, sg], writes=[t1b])
                    B.op("vector", lambda e, j=j, ei=ei: e.scalar_tensor_tensor(
                        out=swb[:], in0=xg[:],
                        scalar=t["W8"][:, j, ei:ei + 1], in1=t1b[:],
                        op0=OP.mult, op1=OP.mult),
                        reads=[xg, (t["W8"], j), t1b], writes=[swb])

                    def swtp(e):
                        for c in range(4):
                            inst = e.transpose(tpE[:, c * P:(c + 1) * P],
                                               swb[:, c * P:(c + 1) * P],
                                               t["ident"][:])
                        return inst
                    B.op("tensor", swtp, reads=[swb, t["ident"]], writes=[tpE])
                    B.op("vector", lambda e: e.tensor_copy(
                        swT[:, :, :], tpE[:].rearrange("p (c q) -> p c q", q=P)),
                        reads=[tpE], writes=[swT])

                    def ymm(e, par=par):
                        for oc in range(2):
                            for c in range(4):
                                inst = e.matmul(
                                    psY[:, oc * 512:(oc + 1) * 512],
                                    swT[:, c, :],
                                    w2b[par][:, c, oc * 512:(oc + 1) * 512],
                                    start=(c == 0), stop=(c == 3))
                        return inst
                    B.op("tensor", ymm, reads=[swT, w2b[par]], writes=[psY])
                    B.op("vector", lambda e, j=j: e.tensor_tensor(
                        out=t["y_acc"][:, j, :], in0=t["y_acc"][:, j, :],
                        in1=psY[:], op=OP.add),
                        reads=[(t["y_acc"], j), psY],
                        writes=[(t["y_acc"], j)]) if (ei + q > 0) else                     B.op("vector", lambda e, j=j: e.tensor_copy(
                        t["y_acc"][:, j, :], psY[:]),
                        reads=[psY], writes=[(t["y_acc"], j)])
        if self.cfg.get("tap_moe"):
            km = self.newsem()
            B.op("sync", lambda e: e.dma_start(
                out=self.dram["dbg_w8"][:, :, :], in_=t["W8"][:, :, :]),
                reads=[t["W8"]], sem=km, inc=16)
            km2 = self.newsem()
            B.op("gpsimd", lambda e: e.dma_start(
                out=self.dram["dbg_yacc"][:, :, :], in_=t["y_acc"][:, :, :]),
                reads=[t["y_acc"]], sem=km2, inc=16)
            self.final_waits = getattr(self, "final_waits", [])
            self.final_waits += [(km, 16), (km2, 16)]
        utmp = self.sb("e_utmp", (P, D), F32, ctxE1)
        for j in range(16):
            ti = 16 * half + j
            xa, xkey = self.xtile(ti)

            def b2mm(e, j=j):
                for oc in range(2):
                    inst = e.matmul(psY[:, oc * 512:(oc + 1) * 512],
                                    t["weT"][:, j, :],
                                    t["b2all"][:, oc * 512:(oc + 1) * 512],
                                    start=True, stop=True)
                return inst
            B.op("tensor", b2mm, reads=[(t["weT"], j), t["b2all"]],
                 writes=[psY])
            B.op("vector", lambda e, xa=xa, ti=ti: e.tensor_scalar(
                out=utmp[:], in0=xa, scalar1=t["negm"][:, ti:ti + 1],
                scalar2=t["rstd"][:, ti:ti + 1], op0=OP.add, op1=OP.mult),
                reads=[xkey, (t["negm"], ti), (t["rstd"], ti)], writes=[utmp])
            B.op("vector", lambda e, xa=xa: e.tensor_tensor(
                out=xa, in0=xa, in1=utmp[:], op=OP.add),
                reads=[xkey, utmp], writes=[xkey])
            B.op("vector", lambda e, xa=xa, j=j: e.tensor_tensor(
                out=xa, in0=xa, in1=t["y_acc"][:, j, :], op=OP.add),
                reads=[xkey, (t["y_acc"], j)], writes=[xkey])
            B.op("vector", lambda e, xa=xa: e.tensor_tensor(
                out=xa, in0=xa, in1=psY[:], op=OP.add),
                reads=[xkey, psY], writes=[xkey])
        self.barrier()
        ctxE1.close()
        ctxE.close()

    def tap_stream(self, name):
        B, t = self.B, self.t
        half = self.cfg["half"]
        rr = self.dram[name].rearrange("(i p) d -> p i d", p=P)
        ks, kg = self.newsem(), self.newsem()
        B.op("sync", lambda e: e.dma_start(
            out=rr[:, 16 * half:16 * half + 16, :], in_=t["x_own"][:, :, :]),
            reads=[t["x_own"]], sem=ks, inc=16)
        B.op("gpsimd", lambda e: e.dma_start(
            out=rr[:, 16 * (1 - half):16 * (1 - half) + 16, :],
            in_=t["x_oth"][:, :, :]), reads=[t["x_oth"]], sem=kg, inc=16)
        self.final_waits = getattr(self, "final_waits", [])
        self.final_waits += [(ks, 16), (kg, 16)]

    def finalize_output(self):
        B, t = self.B, self.t
        ks = self.newsem()
        B.op("sync", lambda e: e.dma_start(
            out=self.dram["out"].rearrange("(i p) d -> p i d", p=P),
            in_=t["x_own"][:, :, :]), reads=[t["x_own"]], sem=ks, inc=16)
        for s, v in getattr(self, "final_waits", []):
            B.wait("sync", s, v)
        B.wait("sync", ks, 16)

    def build(self):
        self.declare_io()
        self.setup()
        self.phase_A()
        self.phase_B()
        self.phase_C()
        self.phase_D()
        self.phase_E()
        self.finalize_output()
        self.B.emit()
        return self.nc


def host_inputs(inputs, b, half):
    def T(a):
        return np.ascontiguousarray(np.asarray(a).T)

    m = {
        "x": np.ascontiguousarray(np.asarray(inputs["L0_attn_x"][b])),
        "wqkvT": T(inputs["L0_attn_w_qkv"]),
        "woutT": T(inputs["L0_attn_w_out"]),
        "b_out": np.asarray(inputs["L0_attn_b_out"]).reshape(1, D),
        "ar": np.asarray(inputs["L0_ffn_adaptive_range"]).reshape(1, 1),
        "Wq": np.asarray(inputs["L1_attn_Wq"]).reshape(D, 8, P).transpose(1, 0, 2),
        "Wk": np.asarray(inputs["L1_attn_Wk"]).reshape(D, 8, P).transpose(1, 0, 2),
        "Wv": np.asarray(inputs["L1_attn_Wv"]).reshape(D, 8, P).transpose(1, 0, 2),
        "Wo": np.asarray(inputs["L1_attn_Wo"]),
        "bo_row": np.asarray(inputs["L1_attn_bo"]).reshape(1, D),
        "norm_scale": np.asarray(inputs["L1_ffn_norm_scale"]).reshape(1, D),
        "gate_w": np.asarray(inputs["L1_ffn_gate_w"]),
        "gate_b": np.asarray(inputs["L1_ffn_gate_b"]).reshape(8, 1),
        "mlp1_w": np.asarray(inputs["L1_ffn_mlp1_w"]),
        "mlp1_b": np.asarray(inputs["L1_ffn_mlp1_b"]),
        "mlp2_w": np.asarray(inputs["L1_ffn_mlp2_w"]),
        "mlp2_b": np.asarray(inputs["L1_ffn_mlp2_b"]),
    }
    for i in range(1, 4):
        m[f"cn_w{i}"] = np.asarray(inputs[f"L0_ffn_cn_w{i}"])
    for i in range(1, 5):
        m[f"cn_b{i}"] = np.asarray(inputs[f"L0_ffn_cn_b{i}"]).reshape(1, -1)
    m["cn_w4c"] = np.asarray(inputs["L0_ffn_cn_w4"]).reshape(2, P).T
    m["gc_w3c"] = np.asarray(inputs["L0_ffn_gc_w3"]).reshape(2, P).T
    for i in range(1, 3):
        m[f"gc_w{i}"] = np.asarray(inputs[f"L0_ffn_gc_w{i}"])
    m["gc_b1c"] = np.asarray(inputs["L0_ffn_gc_b1"]).reshape(4, P).T
    m["gc_b2c"] = np.asarray(inputs["L0_ffn_gc_b2"]).reshape(2, P).T
    m["gc_b3"] = np.asarray(inputs["L0_ffn_gc_b3"]).reshape(1, 1)
    for nm in ("bq", "bk"):
        m[nm] = np.ascontiguousarray(
            np.asarray(inputs["L1_attn_" + nm]).reshape(8, P).T)
    m["bv_row"] = np.asarray(inputs["L1_attn_bv"]).reshape(1, D)
    return {k: np.ascontiguousarray(v) for k, v in m.items()}


from concourse.bass_utils import run_bass_kernel_spmd

_BUILT = {}


def _get_program():
    if "nc" not in _BUILT:
        m = MK({"half": 0})
        _BUILT["nc"] = m.build()
    return _BUILT["nc"]


def kernel(**inputs):
    nc = _get_program()
    base = host_inputs(inputs, 0, 0)
    x_all = np.asarray(inputs["L0_attn_x"], dtype=np.float32)
    in_maps = []
    for c in range(8):
        b, half = c // 2, c % 2
        im = dict(base)
        xb = x_all[b]
        if half == 1:
            xb = np.concatenate([xb[NOWN:], xb[:NOWN]], axis=0)
        im["x"] = np.ascontiguousarray(xb)
        in_maps.append(im)
    res = run_bass_kernel_spmd(nc, in_maps, core_ids=list(range(8)))
    out = np.empty((4, NFULL, D), dtype=np.float32)
    for c in range(8):
        b, half = c // 2, c % 2
        out[b, half * NOWN:(half + 1) * NOWN] = res.results[c]["out"]
    return out


# version 09 (final)
# speedup vs baseline: 1.0844x; 1.0844x over previous
import numpy as np
from contextlib import ExitStack

import concourse.bass as bass
import concourse.mybir as mybir

F32 = mybir.dt.float32
BF16 = mybir.dt.bfloat16
AF = mybir.ActivationFunctionType
OP = mybir.AluOpType
P = 128
D = 1024
NFULL = 4096
NT = 32
NOWN = 2048
HEADS = 16
HD = 64

ENGS = ("sync", "tensor", "vector", "scalar", "gpsimd")


class Sem:
    __slots__ = ("h", "n", "name")

    def __init__(self, nc, ctx, name):
        self.h = ctx.enter_context(nc.semaphore(name))
        self.n = 0
        self.name = name


class Builder:
    def __init__(self, nc, esem):
        self.nc = nc
        self.esem = esem
        self.steps = {e: [] for e in ENGS}
        self.phase = "init"
        self.mark = {e: {} for e in ENGS}
        self.wr = {}
        self.rd = {}

    def wait(self, eng, sem, val=None):
        v = sem.n if val is None else val
        if v <= 0 or self.mark[eng].get(sem, 0) >= v:
            return
        self.mark[eng][sem] = v
        self.steps[eng].append((self.phase, lambda e, s=sem.h, v=v: e.wait_ge(s, v)))

    def _keys(self, h, tag):
        base = id(h) if not isinstance(h, str) else h
        return (base, tag), (base, None)

    def op(self, eng, fn, reads=(), writes=(), sem=None, inc=1, extra_inc=()):
        def norm(lst):
            out = []
            for it in lst:
                if isinstance(it, tuple):
                    out.append(it)
                else:
                    out.append((it, None))
            return out
        reads, writes = norm(reads), norm(writes)
        for h, tag in reads:
            for key in self._keys(h, tag) if tag is not None else [(id(h) if not isinstance(h, str) else h, None)]:
                w = self.wr.get(key)
                if w:
                    self.wait(eng, w[0], w[1])
            if tag is None:
                base = id(h) if not isinstance(h, str) else h
                for key, w in list(self.wr.items()):
                    if key[0] == base and key[1] is not None:
                        self.wait(eng, w[0], w[1])
        for h, tag in writes:
            base = id(h) if not isinstance(h, str) else h
            keys = [(base, tag)] if tag is not None else [k for k in set(list(self.wr) + list(self.rd)) if k[0] == base] + [(base, None)]
            if tag is not None:
                keys.append((base, None))
            for key in set(keys):
                w = self.wr.get(key)
                if w:
                    self.wait(eng, w[0], w[1])
                for s, v in self.rd.get(key, {}).items():
                    self.wait(eng, s, v)
        s = sem or self.esem[eng]
        s.n += inc
        val = s.n
        for es, k in extra_inc:
            es.n += k
        incs = tuple([(s, inc)] + list(extra_inc))

        def wrapped(e, fn=fn, incs=incs):
            inst = fn(e)
            for ss, k in incs:
                inst.then_inc(ss.h, k)
        self.steps[eng].append((self.phase, wrapped))
        for h, tag in writes:
            base = id(h) if not isinstance(h, str) else h
            if tag is None:
                for key in [k for k in list(self.wr) if k[0] == base]:
                    del self.wr[key]
                for key in [k for k in list(self.rd) if k[0] == base]:
                    del self.rd[key]
            self.wr[(base, tag)] = (s, val)
            self.rd.pop((base, tag), None)
        for h, tag in reads:
            base = id(h) if not isinstance(h, str) else h
            self.rd.setdefault((base, tag), {})[s] = val
        return (s, val)

    def barrier(self, extra_sems=()):
        sems = list(self.esem.values()) + list(extra_sems)
        for eng in ENGS:
            for s in sems:
                self.wait(eng, s, s.n)
        self.wr.clear()
        self.rd.clear()

    def emit(self):
        nc = self.nc

        def run(e, steps):
            i = 0
            while i < len(steps):
                ph = steps[i][0]
                j = i
                with nc.named_scope(ph):
                    while j < len(steps) and steps[j][0] == ph:
                        steps[j][1](e)
                        j += 1
                i = j
        with self.nc.Block() as blk:
            for eng in ENGS:
                steps = self.steps[eng]
                if not steps:
                    continue
                getattr(blk, eng)(lambda e, steps=steps: run(e, steps))


class MK:
    def __init__(self, cfg):
        self.cfg = cfg
        self.nc = bass.Bass("TRN2", target_bir_lowering=False)
        self.ctx = ExitStack()
        nc, ctx = self.nc, self.ctx
        esem = {e: Sem(nc, ctx, "c_" + e) for e in ENGS}
        self.sem_pool = [Sem(nc, ctx, f"d{i}") for i in range(24)]
        self._sem_i = 0
        self.B = Builder(nc, esem)
        self.dram = {}
        self.t = {}

    def newsem(self):
        s = self.sem_pool[self._sem_i]
        self._sem_i += 1
        return s

    def barrier(self):
        self.B.barrier([s for s in self.sem_pool if s.n > 0])

    def seal_group(self, sem, handles):
        for h in handles:
            if isinstance(h, tuple):
                base = id(h[0]) if not isinstance(h[0], str) else h[0]
                key = (base, h[1])
            else:
                key = (id(h) if not isinstance(h, str) else h, None)
            self.B.wr[key] = (sem, sem.n)

    def dramt(self, name, shape, dtype=F32, kind="ExternalInput"):
        h = self.nc.dram_tensor(name, list(shape), dtype, kind=kind)
        self.dram[name] = h.ap()
        return self.dram[name]

    def sb(self, name, shape, dtype, ctx=None, side=None):
        if side:
            h = (ctx or self.ctx).enter_context(
                self.nc.sbuf_tensor(name, list(shape), dtype, side=side))
        else:
            h = (ctx or self.ctx).enter_context(
                self.nc.sbuf_tensor(name, list(shape), dtype))
        self.t[name] = h
        return h

    def ps(self, name, shape, dtype, ctx=None):
        h = (ctx or self.ctx).enter_context(
            self.nc.psum_tensor(name, list(shape), dtype))
        self.t[name] = h
        return h

    def load(self, eng, dst_ap, src_ap, bufkey, sem):
        self.B.op(eng, lambda e, d=dst_ap, s=src_ap: e.dma_start(out=d, in_=s),
                  writes=[bufkey], sem=sem, inc=16)

    def declare_io(self):
        d = self.dramt
        d("x", (NFULL, D))
        d("wqkvT", (D, 3 * D))
        d("woutT", (D, D))
        d("b_out", (1, D))
        for i, (a, b) in enumerate([(D, D), (D, 512), (512, 256), (256, 1)]):
            if b > 1:
                d(f"cn_w{i+1}", (a, b))
            d(f"cn_b{i+1}", (1, b))
        d("cn_w4c", (P, 2))
        d("gc_w1", (D, 512))
        d("gc_w2", (512, 256))
        d("gc_w3c", (P, 2))
        d("gc_b1c", (P, 4))
        d("gc_b2c", (P, 2))
        d("gc_b3", (1, 1))
        d("ar", (1, 1))
        for w in ("Wq", "Wk", "Wv"):
            d(w, (8, D, P))
        d("Wo", (D, D))
        for b in ("bq", "bk"):
            d(b, (P, 8))
        d("bv_row", (1, D))
        d("bo_row", (1, D))
        d("norm_scale", (1, D))
        d("gate_w", (D, 8))
        d("gate_b", (8, 1))
        d("mlp1_w", (8, D, 4096))
        d("mlp1_b", (8, 4096))
        d("mlp2_w", (8, 2048, D))
        d("mlp2_b", (8, D))
        d("out", (NOWN, D), kind="ExternalOutput")
        for tap in ("tap_xmid", "tap_l0"):
            if self.cfg.get(tap):
                d("dbg_" + tap[4:], (NFULL, D), kind="ExternalOutput")
        if self.cfg.get("tap_l1a"):
            d("dbg_l1a", (NOWN, D), kind="ExternalOutput")
        if self.cfg.get("tap_moe"):
            d("dbg_w8", (P, 16, 8), kind="ExternalOutput")
            d("dbg_yacc", (P, 16, D), kind="ExternalOutput")
        if self.cfg.get("tap_sel"):
            d("dbg_raw0", (P, NFULL), kind="ExternalOutput")
            d("dbg_raw1", (P, NFULL), kind="ExternalOutput")
            d("dbg_selc", (P, 48), kind="ExternalOutput")
            d("dbg_den8", (P, 8), kind="ExternalOutput")
            d("dbg_w0", (P, NFULL), kind="ExternalOutput")
            d("dbg_w1", (P, NFULL), kind="ExternalOutput")
        if self.cfg.get("tap_atn") is not None:
            d("dbg_atnT", (P, 8, NOWN // 2), kind="ExternalOutput")

    def setup(self):
        B, t = self.B, self.t
        self.B.phase = "setup"
        self.sb("x_own", (P, 16, D), F32)
        self.ctx_oth = ExitStack()
        self.sb("x_oth", (P, 16, D), BF16, self.ctx_oth, side="right")
        self.sb("ident", (P, P), BF16)
        self.sb("identf", (P, P), F32)
        self.sb("ones_row", (1, P), BF16)
        self.sb("zero_row", (1, 512), BF16)
        self.sb("onecol_bf", (P, 1), BF16)
        self.sb("negm", (P, NT), F32)
        self.sb("rstd", (P, NT), F32)
        self.sb("stat", (P, 16), F32)
        self.sb("epscol", (P, 1), F32)

        x = self.dram["x"]
        half = self.cfg["half"]
        xr = x.rearrange("(i p) d -> p i d", p=P)
        self.load("sync", t["x_own"][:, :, :],
                  xr[:, 16 * half:16 * half + 16, :], t["x_own"], self.newsem())
        self.load("gpsimd", t["x_oth"][:, :, :],
                  xr[:, 16 * (1 - half):16 * (1 - half) + 16, :], t["x_oth"],
                  self.newsem())
        for name, v in [("epscol", 1e-5), ("ones_row", 1.0),
                        ("zero_row", 0.0), ("onecol_bf", 1.0)]:
            B.op("gpsimd", lambda e, n=name, v=v: e.memset(t[n][:], v),
                 writes=[t[name]])
        for name in ("ident", "identf"):
            B.op("gpsimd", lambda e, n=name: e.memset(t[n][:], 0.0),
                 writes=[t[name]])
            B.op("gpsimd", lambda e, n=name: e.affine_select(
                out=t[n][:], in_=t[n][:], compare_op=OP.not_equal,
                fill=1.0, base=0, pattern=[[-1, P]], channel_multiplier=1),
                reads=[t[name]], writes=[t[name]])

    def xtile(self, i):
        half = self.cfg["half"]
        lo = 16 * half
        if lo <= i < lo + 16:
            return self.t["x_own"][:, i - lo, :], (self.t["x_own"], i - lo)
        j = i - 16 * (1 - half)
        return self.t["x_oth"][:, j, :], (self.t["x_oth"], j)

    def ln_stats(self, i, negm, rstd):
        B, t = self.B, self.t
        xa, xkey = self.xtile(i)
        c0 = 8 * (i % 2)
        st = t["stat"]
        B.op("scalar", lambda e, xa=xa, c0=c0, jk=self.junk: e.activation(
            jk[:], xa, AF.Square, accum_out=st[:, c0:c0 + 1]),
            reads=[xkey], writes=[(st, c0), self.junk])
        B.op("scalar", lambda e, xa=xa, c0=c0, jk=self.junk: e.activation(
            jk[:], xa, AF.Identity, accum_out=st[:, c0 + 1:c0 + 2]),
            reads=[xkey], writes=[(st, c0 + 1), self.junk])
        B.op("vector", lambda e, c0=c0, i=i: e.tensor_scalar(
            out=negm[:, i:i + 1], in0=st[:, c0 + 1:c0 + 2], scalar1=-1.0 / D,
            scalar2=None, op0=OP.mult),
            reads=[(st, c0 + 1)], writes=[(negm, i)])
        B.op("vector", lambda e, c0=c0, i=i: e.tensor_tensor(
            out=st[:, c0 + 2:c0 + 3], in0=negm[:, i:i + 1], in1=negm[:, i:i + 1],
            op=OP.mult), reads=[(negm, i)], writes=[(st, c0 + 2)])
        B.op("vector", lambda e, c0=c0: e.tensor_scalar(
            out=st[:, c0 + 3:c0 + 4], in0=st[:, c0:c0 + 1], scalar1=1.0 / D,
            scalar2=None, op0=OP.mult),
            reads=[(st, c0)], writes=[(st, c0 + 3)])
        B.op("vector", lambda e, c0=c0: e.tensor_tensor(
            out=st[:, c0 + 4:c0 + 5], in0=st[:, c0 + 3:c0 + 4],
            in1=st[:, c0 + 2:c0 + 3], op=OP.subtract),
            reads=[(st, c0 + 3), (st, c0 + 2)], writes=[(st, c0 + 4)])
        B.op("scalar", lambda e, c0=c0: e.activation(
            st[:, c0 + 5:c0 + 6], st[:, c0 + 4:c0 + 5], AF.Sqrt,
            bias=t["epscol"][:, 0:1]),
            reads=[(st, c0 + 4), t["epscol"]], writes=[(st, c0 + 5)])
        B.op("vector", lambda e, c0=c0, i=i: e.reciprocal(
            rstd[:, i:i + 1], st[:, c0 + 5:c0 + 6]),
            reads=[(st, c0 + 5)], writes=[(rstd, i)])

    def ln_apply(self, i, negm, rstd, out_bf):
        B = self.B
        xa, xkey = self.xtile(i)
        B.op("vector", lambda e, xa=xa, i=i: e.tensor_scalar(
            out=out_bf[:], in0=xa, scalar1=negm[:, i:i + 1],
            scalar2=rstd[:, i:i + 1], op0=OP.add, op1=OP.mult),
            reads=[xkey, (negm, i), (rstd, i)], writes=[out_bf])

    def transpose_tile(self, src, tp_ps, dst, nblk=8):
        B, t = self.B, self.t

        def f(e, src=src, tp=tp_ps, nblk=nblk):
            for c in range(nblk):
                inst = e.transpose(tp[:, c * P:(c + 1) * P],
                                   src[:, c * P:(c + 1) * P], t["ident"][:])
            return inst
        B.op("tensor", f, reads=[src, t["ident"]], writes=[tp_ps])
        B.op("vector", lambda e, tp=tp_ps, dst=dst: e.tensor_copy(dst[:, :, :], tp[:]),
             reads=[tp_ps], writes=[dst])

    def phase_A(self):
        B, t = self.B, self.t
        self.B.phase = "A"
        ctxAB = self.ctxAB = ExitStack()
        self.sb("wqkvT_sb", (P, 8, 3 * D), BF16, ctxAB)
        self.sb("woutT_sb", (P, 8, D), BF16, ctxAB)
        self.sb("kv_sb", (P, D), BF16, ctxAB)
        self.sb("ksumB", (P, 8, 16), BF16, ctxAB)
        self.sb("bout_bf", (1, D), BF16, ctxAB)

        self.load("gpsimd", t["wqkvT_sb"][:, :, :],
                  self.dram["wqkvT"].rearrange("(c p) o -> p c o", p=P),
                  t["wqkvT_sb"], self.newsem())
        self.load("gpsimd", t["woutT_sb"][:, :, :],
                  self.dram["woutT"].rearrange("(c p) o -> p c o", p=P),
                  t["woutT_sb"], self.newsem())
        self.load("gpsimd", t["bout_bf"][:], self.dram["b_out"][:],
                  t["bout_bf"], self.newsem())

        ctxA = ExitStack()
        self.junk = self.sb("junkA", (P, D), BF16, ctxA)
        xn = [self.sb(f"a_xn{p}", (P, D), BF16, ctxA) for p in range(2)]
        xnT = [self.sb(f"a_xnT{p}", (P, 8, P), BF16, ctxA) for p in range(2)]
        abuf = [self.sb(f"a_ab{p}", (P, D), BF16, ctxA) for p in range(2)]
        ebuf = [self.sb(f"a_eb{p}", (P, D), BF16, ctxA) for p in range(2)]
        kpr = [self.sb(f"a_kp{p}", (P, D), BF16, ctxA) for p in range(2)]
        vbf = [self.sb(f"a_vb{p}", (P, D), BF16, ctxA) for p in range(2)]
        tp_ps = self.ps("a_tp", (P, D), BF16, ctxA)
        kvproj = self.ps("a_kvproj", (P, 2048), F32, ctxA)
        kv_ps = self.ps("a_kv", (P, D), F32, ctxA)

        def opener(e):
            e.matmul(kv_ps[:, 0:512], t["zero_row"][:, 0:P], t["zero_row"][:],
                     start=True, stop=False)
            return e.matmul(kv_ps[:, 512:1024], t["zero_row"][:, 0:P],
                            t["zero_row"][:], start=True, stop=False)
        B.op("tensor", opener, reads=[t["zero_row"]], writes=[(kv_ps, "open")])

        for i in range(NT):
            par = i % 2
            self.ln_stats(i, t["negm"], t["rstd"])
            self.ln_apply(i, t["negm"], t["rstd"], xn[par])
            self.transpose_tile(xn[par], tp_ps, xnT[par])

            def kvproj_mm(e, par=par):
                for nb in range(4):
                    for c in range(8):
                        inst = e.matmul(
                            kvproj[:, nb * 512:(nb + 1) * 512],
                            xnT[par][:, c, :],
                            t["wqkvT_sb"][:, c, D + nb * 512:D + (nb + 1) * 512],
                            start=(c == 0), stop=(c == 7))
                return inst
            B.op("tensor", kvproj_mm, reads=[xnT[par], t["wqkvT_sb"]],
                 writes=[kvproj])
            B.op("vector", lambda e, par=par: e.tensor_scalar(
                out=abuf[par][:], in0=kvproj[:, 0:D], scalar1=0.0, scalar2=None,
                op0=OP.min), reads=[kvproj], writes=[abuf[par]])
            B.op("scalar", lambda e, par=par: e.activation(
                ebuf[par][:], abuf[par][:], AF.Exp),
                reads=[abuf[par]], writes=[ebuf[par]])
            B.op("vector", lambda e, par=par: e.scalar_tensor_tensor(
                out=kpr[par][:], in0=kvproj[:, 0:D], scalar=0.0, in1=ebuf[par][:],
                op0=OP.max, op1=OP.add),
                reads=[kvproj, ebuf[par]], writes=[kpr[par]])
            B.op("vector", lambda e, par=par: e.tensor_copy(
                vbf[par][:], kvproj[:, D:2 * D]),
                reads=[kvproj], writes=[vbf[par]])

            def kv_mm(e, par=par):
                for h in range(HEADS):
                    rp = 64 * (h % 2)
                    cp = 128 * (h // 2)
                    e.matmul(kv_ps[rp:rp + 64, cp:cp + 64],
                             kpr[par][:, 64 * h:64 * h + 64],
                             vbf[par][:, 64 * h:64 * h + 64],
                             start=False, stop=False, tile_position=(0, rp))
                    inst = e.matmul(kv_ps[rp:rp + 64, cp + 64:cp + 65],
                                    kpr[par][:, 64 * h:64 * h + 64],
                                    t["onecol_bf"][:],
                                    start=False, stop=False, tile_position=(0, rp))
                return inst
            B.op("tensor", kv_mm, reads=[kpr[par], vbf[par], t["onecol_bf"],
                                         (kv_ps, "open")], writes=[(kv_ps, i)])

        def closer(e):
            e.matmul(kv_ps[:, 0:512], t["zero_row"][:, 0:P], t["zero_row"][:],
                     start=False, stop=True)
            return e.matmul(kv_ps[:, 512:1024], t["zero_row"][:, 0:P],
                            t["zero_row"][:], start=False, stop=True)
        B.op("tensor", closer, reads=[t["zero_row"]], writes=[kv_ps])
        B.op("vector", lambda e: e.tensor_copy(t["kv_sb"][:], kv_ps[:]),
             reads=[kv_ps], writes=[t["kv_sb"]])
        B.op("vector", lambda e: e.memset(t["ksumB"][:], 0.0),
             writes=[t["ksumB"]])

        def ksum_copy(e):
            for h in range(HEADS):
                rp = 64 * (h % 2)
                cp = 128 * (h // 2)
                inst = e.tensor_copy(
                    self.t["ksumB"][rp:rp + 64, h // 2, h:h + 1],
                    kv_ps[rp:rp + 64, cp + 64:cp + 65])
            return inst
        B.op("vector", ksum_copy, reads=[kv_ps, t["ksumB"]], writes=[t["ksumB"]])
        self.barrier()
        ctxA.close()

    def phase_B(self):
        B, t = self.B, self.t
        self.B.phase = "B"
        ctxB = ExitStack()
        self.sb("selz", (16, 16, 64), BF16, ctxB)
        B.op("gpsimd", lambda e: e.memset(t["selz"][:, :, :], 0.0),
             writes=[t["selz"]])

        def selz_fill(e):
            for h in range(16):
                inst = e.affine_select(
                    out=t["selz"][:, h, :], in_=t["selz"][:, h, :],
                    compare_op=OP.not_equal, fill=1.0, base=-h,
                    pattern=[[0, 64]], channel_multiplier=1)
            return inst
        B.op("gpsimd", selz_fill, reads=[t["selz"]], writes=[t["selz"]])
        self.junk = self.sb("junkB", (P, D), BF16, ctxB)
        xn = [self.sb(f"b_xn{p}", (P, D), BF16, ctxB) for p in range(2)]
        xnT = [self.sb(f"b_xnT{p}", (P, 8, P), BF16, ctxB) for p in range(2)]
        abuf = [self.sb("b_ab0", (P, 8, P), BF16, ctxB)] * 2
        ebuf = [self.sb("b_eb0", (P, 8, P), BF16, ctxB)] * 2
        qpr = [self.sb(f"b_qp{p}", (P, 8, P), BF16, ctxB) for p in range(2)]
        atn = [self.sb(f"b_at{p}", (P, 8, P), BF16, ctxB) for p in range(2)]
        zsb = [self.sb(f"b_z{p}", (16, P), BF16, ctxB) for p in range(2)]
        ztmp = self.sb("b_ztmp", (16, P), F32, ctxB)
        zbsb = [self.sb("b_zbs0", (P, 8, P), BF16, ctxB)] * 2
        tp_ps = self.ps("b_tp", (P, D), BF16, ctxB)
        psA = self.ps("b_psA", (P, 8, P), F32, ctxB)
        zpre = self.ps("b_zpre", (P, 512), F32, ctxB)
        zb_ps = self.ps("b_zb", (P, 8, P), F32, ctxB)
        y_ps = self.ps("b_y", (P, D), F32, ctxB)

        for i in range(NT):
            par = i % 2
            self.ln_apply(i, t["negm"], t["rstd"], xn[par])
            self.transpose_tile(xn[par], tp_ps, xnT[par])

            def qT_mm(e, par=par):
                for qc in range(8):
                    for c in range(8):
                        inst = e.matmul(
                            psA[:, qc, :],
                            t["wqkvT_sb"][:, c, qc * P:(qc + 1) * P],
                            xnT[par][:, c, :],
                            start=(qc % 4 == 0 and c == 0),
                            stop=(qc % 4 == 3 and c == 7))
                return inst
            B.op("tensor", qT_mm, reads=[xnT[par], t["wqkvT_sb"]], writes=[psA])
            B.op("vector", lambda e, par=par: e.tensor_scalar(
                out=abuf[par][:, :, :], in0=psA[:, :, :], scalar1=0.0,
                scalar2=None, op0=OP.min), reads=[psA], writes=[abuf[par]])
            B.op("scalar", lambda e, par=par: e.activation(
                ebuf[par][:, :, :], abuf[par][:, :, :], AF.Exp),
                reads=[abuf[par]], writes=[ebuf[par]])
            B.op("vector", lambda e, par=par: e.scalar_tensor_tensor(
                out=qpr[par][:, :, :], in0=psA[:, :, :], scalar=0.0,
                in1=ebuf[par][:, :, :], op0=OP.max, op1=OP.add),
                reads=[psA, ebuf[par]], writes=[qpr[par]])

            def zpre_mm(e, par=par):
                for c in range(8):
                    inst = e.matmul(zpre[0:16, 0:P], t["ksumB"][:, c, :],
                                    qpr[par][:, c, :],
                                    start=(c == 0), stop=(c == 7))
                return inst
            B.op("tensor", zpre_mm, reads=[qpr[par], t["ksumB"]], writes=[zpre])
            B.op("vector", lambda e: e.tensor_scalar(
                out=ztmp[:], in0=zpre[0:16, 0:P], scalar1=1e-6, scalar2=None,
                op0=OP.add), reads=[zpre], writes=[ztmp])
            def zrec(e, par=par):
                with self.nc.allow_low_precision(reason="z broadcast in bf16"):
                    return e.reciprocal(zsb[par][:], ztmp[:])
            B.op("vector", zrec, reads=[ztmp], writes=[zsb[par]])

            def zb_mm(e, par=par):
                e.matmul(zb_ps[:, 0:4, :], t["zero_row"][:, 0:P],
                         t["zero_row"][:], start=True, stop=False)
                e.matmul(zb_ps[:, 4:8, :], t["zero_row"][:, 0:P],
                         t["zero_row"][:], start=True, stop=False)
                for h in range(HEADS):
                    rp = 64 * (h % 2)
                    inst = e.matmul(
                        zb_ps[rp:rp + 64, h // 2, :],
                        t["selz"][:, h, :],
                        zsb[par][:, :],
                        start=False, stop=False,
                        tile_position=(0, rp))
                e.matmul(zb_ps[:, 0:4, :], t["zero_row"][:, 0:P],
                         t["zero_row"][:], start=False, stop=True)
                inst = e.matmul(zb_ps[:, 4:8, :], t["zero_row"][:, 0:P],
                                t["zero_row"][:], start=False, stop=True)
                return inst
            B.op("tensor", zb_mm, reads=[zsb[par], t["selz"]], writes=[zb_ps])
            B.op("vector", lambda e, par=par: e.tensor_copy(
                zbsb[par][:, :, :], zb_ps[:, :, :]),
                reads=[zb_ps], writes=[zbsb[par]])

            def o_mm(e, par=par):
                e.matmul(psA[:, 0:4, :], t["zero_row"][:, 0:P],
                         t["zero_row"][:], start=True, stop=False)
                e.matmul(psA[:, 4:8, :], t["zero_row"][:, 0:P],
                         t["zero_row"][:], start=True, stop=False)
                for h in range(HEADS):
                    rp = 64 * (h % 2)
                    inst = e.matmul(
                        psA[rp:rp + 64, h // 2, :],
                        t["kv_sb"][rp:rp + 64, 128 * (h // 2):128 * (h // 2) + 64],
                        qpr[par][rp:rp + 64, h // 2, :],
                        start=False, stop=False,
                        tile_position=(rp, rp))
                e.matmul(psA[:, 0:4, :], t["zero_row"][:, 0:P],
                         t["zero_row"][:], start=False, stop=True)
                inst = e.matmul(psA[:, 4:8, :], t["zero_row"][:, 0:P],
                                t["zero_row"][:], start=False, stop=True)
                return inst
            B.op("tensor", o_mm, reads=[t["kv_sb"], qpr[par]], writes=[psA])
            B.op("vector", lambda e, par=par: e.tensor_tensor(
                out=atn[par][:, :, :], in0=psA[:, :, :], in1=zbsb[par][:, :, :],
                op=OP.mult), reads=[psA, zbsb[par]], writes=[atn[par]])

            def y_mm(e, par=par):
                for oc in range(2):
                    e.matmul(y_ps[:, oc * 512:(oc + 1) * 512], t["ones_row"][:],
                             t["bout_bf"][:, oc * 512:(oc + 1) * 512],
                             start=True, stop=False)
                    for c in range(8):
                        inst = e.matmul(
                            y_ps[:, oc * 512:(oc + 1) * 512],
                            atn[par][:, c, :],
                            t["woutT_sb"][:, c, oc * 512:(oc + 1) * 512],
                            start=False, stop=(c == 7))
                return inst
            B.op("tensor", y_mm,
                 reads=[atn[par], t["woutT_sb"], t["ones_row"], t["bout_bf"]],
                 writes=[y_ps])
            xa, xkey = self.xtile(i)
            B.op("vector", lambda e, xa=xa: e.tensor_tensor(
                out=xa, in0=xa, in1=y_ps[:], op=OP.add),
                reads=[xkey, y_ps], writes=[xkey])

        self.barrier()
        ctxB.close()
        if self.cfg.get("tap_xmid"):
            self.tap_stream("dbg_xmid")

    def phase_C(self):
        B, t = self.B, self.t
        self.B.phase = "C"
        self.barrier()
        self.ctxAB.close()
        ctxC = ExitStack()
        self.junk = self.sb("junkC", (P, D), BF16, ctxC)
        self.sb("cn1_sb", (P, 8, D), BF16, ctxC)
        self.sb("cn2_sb", (P, 8, 512), BF16, ctxC)
        self.sb("cn3_sb", (P, 4, 256), BF16, ctxC)
        self.sb("cn4_sb", (P, 2), BF16, ctxC)
        self.sb("gc1_sb", (P, 8, 512), BF16, ctxC)
        self.sb("gc2_sb", (P, 4, 256), BF16, ctxC)
        self.sb("gc3_sb", (P, 2), BF16, ctxC)
        self.sb("cnb_rows", (1, 2048), BF16, ctxC)
        self.sb("gcb1c", (P, 4), F32, ctxC)
        self.sb("gcb2c", (P, 2), F32, ctxC)
        self.sb("gcb3", (1, 1), F32, ctxC)
        self.sb("cnb4", (1, 1), F32, ctxC)
        self.sb("ar_sb", (1, 1), F32, ctxC)
        self.sb("cst", (1, 8), F32, ctxC)
        self.sb("gfb", (1, 1), F32, ctxC)
        self.sb("mean_bf", (P, 8), BF16, ctxC)
        self.sb("h1g_bf", (P, 4), BF16, ctxC)
        self.sb("h2g_bf", (P, 2), BF16, ctxC)

        wsem = self.newsem()
        for nm, src in [("cn1_sb", "cn_w1"), ("cn2_sb", "cn_w2"),
                        ("gc1_sb", "gc_w1")]:
            self.load("gpsimd", t[nm][:, :, :],
                      self.dram[src].rearrange("(c p) o -> p c o", p=P),
                      t[nm], wsem)
        for nm, src, nc_ in [("cn3_sb", "cn_w3", 4), ("gc2_sb", "gc_w2", 4)]:
            self.load("gpsimd", t[nm][:, :, :],
                      self.dram[src].rearrange("(c p) o -> p c o", p=P),
                      t[nm], wsem)
        for nm, src in [("cn4_sb", "cn_w4c"), ("gc3_sb", "gc_w3c")]:
            self.load("gpsimd", t[nm][:, :], self.dram[src][:], t[nm], wsem)
        self.load("gpsimd", t["cnb_rows"][:, 0:D], self.dram["cn_b1"][:],
                  (t["cnb_rows"], 1), wsem)
        self.load("gpsimd", t["cnb_rows"][:, D:D + 512], self.dram["cn_b2"][:],
                  (t["cnb_rows"], 2), wsem)
        self.load("gpsimd", t["cnb_rows"][:, D + 512:D + 768],
                  self.dram["cn_b3"][:], (t["cnb_rows"], 3), wsem)
        sy = self.newsem()
        self.load("sync", t["gcb1c"][:], self.dram["gc_b1c"][:], t["gcb1c"], sy)
        self.load("sync", t["gcb2c"][:], self.dram["gc_b2c"][:], t["gcb2c"], sy)
        self.load("sync", t["gcb3"][:], self.dram["gc_b3"][:], t["gcb3"], sy)
        self.load("sync", t["cnb4"][:], self.dram["cn_b4"][:], t["cnb4"], sy)
        self.load("sync", t["ar_sb"][:], self.dram["ar"][:], t["ar_sb"], sy)
        self.seal_group(wsem, [t["cn1_sb"], t["cn2_sb"], t["gc1_sb"],
                               t["cn3_sb"], t["gc2_sb"], t["cn4_sb"],
                               t["gc3_sb"], (t["cnb_rows"], 1),
                               (t["cnb_rows"], 2), (t["cnb_rows"], 3)])
        self.seal_group(sy, [t["gcb1c"], t["gcb2c"],
                             t["gcb3"], t["cnb4"], t["ar_sb"]])

        ctxC1 = ExitStack()
        xn2 = [self.sb(f"c_xn{p}", (P, D), BF16, ctxC1) for p in range(2)]
        gcsum = self.ps("c_gcsum", (P, 512), F32, ctxC1)
        B.op("tensor", lambda e: e.matmul(
            gcsum[:, 0:512], t["zero_row"][:, 0:P], t["zero_row"][:],
            start=True, stop=False), reads=[t["zero_row"]],
            writes=[(gcsum, "open")])
        for i in range(NT):
            par = i % 2
            self.ln_stats(i, t["negm"], t["rstd"])
            self.ln_apply(i, t["negm"], t["rstd"], xn2[par])

            def gcs_mm(e, par=par):
                for c in range(8):
                    inst = e.matmul(gcsum[:, c:c + 1],
                                    xn2[par][:, c * P:(c + 1) * P],
                                    t["onecol_bf"][:],
                                    start=False, stop=False)
                return inst
            B.op("tensor", gcs_mm, reads=[xn2[par], t["onecol_bf"],
                                          (gcsum, "open")], writes=[(gcsum, i)])
        B.op("tensor", lambda e: e.matmul(
            gcsum[:, 0:512], t["zero_row"][:, 0:P], t["zero_row"][:],
            start=False, stop=True), reads=[t["zero_row"]], writes=[gcsum])

        B.op("vector", lambda e: e.tensor_scalar(
            out=t["mean_bf"][:], in0=gcsum[:, 0:8], scalar1=1.0 / NFULL,
            scalar2=None, op0=OP.mult), reads=[gcsum], writes=[t["mean_bf"]])
        h1g = self.ps("c_h1g", (P, 512), F32, ctxC1)
        h2g = self.ps("c_h2g", (P, 512), F32, ctxC1)
        h3g = self.ps("c_h3g", (P, 512), F32, ctxC1)

        def gc1_mm(e):
            for oc in range(4):
                for c in range(8):
                    inst = e.matmul(h1g[:, oc:oc + 1],
                                    t["gc1_sb"][:, c, oc * P:(oc + 1) * P],
                                    t["mean_bf"][:, c:c + 1],
                                    start=(oc == 0 and c == 0),
                                    stop=(oc == 3 and c == 7))
            return inst
        B.op("tensor", gc1_mm, reads=[t["gc1_sb"], t["mean_bf"]], writes=[h1g])

        def gc1_act(e):
            for oc in range(4):
                inst = e.activation(t["h1g_bf"][:, oc:oc + 1], h1g[:, oc:oc + 1],
                                    AF.Gelu, bias=t["gcb1c"][:, oc:oc + 1])
            return inst
        B.op("scalar", gc1_act, reads=[h1g, t["gcb1c"]], writes=[t["h1g_bf"]])

        def gc2_mm(e):
            for oc in range(2):
                for c in range(4):
                    inst = e.matmul(h2g[:, oc:oc + 1],
                                    t["gc2_sb"][:, c, oc * P:(oc + 1) * P],
                                    t["h1g_bf"][:, c:c + 1],
                                    start=(oc == 0 and c == 0),
                                    stop=(oc == 1 and c == 3))
            return inst
        B.op("tensor", gc2_mm, reads=[t["gc2_sb"], t["h1g_bf"]], writes=[h2g])

        def gc2_act(e):
            for oc in range(2):
                inst = e.activation(t["h2g_bf"][:, oc:oc + 1], h2g[:, oc:oc + 1],
                                    AF.Gelu, bias=t["gcb2c"][:, oc:oc + 1])
            return inst
        B.op("scalar", gc2_act, reads=[h2g, t["gcb2c"]], writes=[t["h2g_bf"]])

        def gc3_mm(e):
            for c in range(2):
                inst = e.matmul(h3g[0:1, 0:1], t["gc3_sb"][:, c:c + 1],
                                t["h2g_bf"][:, c:c + 1],
                                start=(c == 0), stop=(c == 1))
            return inst
        B.op("tensor", gc3_mm, reads=[t["gc3_sb"], t["h2g_bf"]], writes=[h3g])
        B.op("scalar", lambda e: e.activation(
            t["gfb"][:], h3g[0:1, 0:1], AF.Sigmoid, bias=t["gcb3"][:]),
            reads=[h3g, t["gcb3"]], writes=[t["gfb"]])
        cst = t["cst"]
        B.op("vector", lambda e: e.tensor_scalar(
            out=cst[:, 4:5], in0=t["gfb"][:], scalar1=1.4, scalar2=None,
            op0=OP.mult), reads=[t["gfb"]], writes=[(cst, 4)])
        B.op("vector", lambda e: e.tensor_tensor(
            out=cst[:, 5:6], in0=cst[:, 4:5], in1=t["ar_sb"][:], op=OP.mult),
            reads=[(cst, 4), t["ar_sb"]], writes=[(cst, 5)])
        B.op("vector", lambda e: e.tensor_scalar(
            out=cst[:, 6:7], in0=cst[:, 5:6], scalar1=0.1, scalar2=None,
            op0=OP.add), reads=[(cst, 5)], writes=[(cst, 6)])
        B.op("vector", lambda e: e.tensor_tensor(
            out=cst[:, 0:1], in0=cst[:, 6:7], in1=t["ar_sb"][:], op=OP.subtract),
            reads=[(cst, 6), t["ar_sb"]], writes=[(cst, 0)])
        B.op("vector", lambda e: e.tensor_scalar(
            out=cst[:, 1:2], in0=t["ar_sb"][:], scalar1=0.6, scalar2=None,
            op0=OP.mult), reads=[t["ar_sb"]], writes=[(cst, 1)])
        B.op("vector", lambda e: e.tensor_scalar(
            out=cst[:, 2:3], in0=cst[:, 0:1], scalar1=-1.0, scalar2=None,
            op0=OP.mult), reads=[(cst, 0)], writes=[(cst, 2)])
        B.op("vector", lambda e: e.tensor_scalar(
            out=cst[:, 3:4], in0=cst[:, 1:2], scalar1=-1.0, scalar2=None,
            op0=OP.mult), reads=[(cst, 1)], writes=[(cst, 3)])
        self.barrier()
        ctxC1.close()

        ctxC2 = ExitStack()
        out_cur = self.sb("c_out0", (P, D), F32, ctxC2)
        out_nxt = self.sb("c_out1", (P, D), F32, ctxC2)
        outT_bf = self.sb("c_outT", (P, 8, P), BF16, ctxC2)
        h1T_bf = self.sb("c_h1T", (P, 8, P), BF16, ctxC2)
        h2T_bf = self.sb("c_h2T", (P, 4, P), BF16, ctxC2)
        h3T_bf = self.sb("c_h3T", (P, 2, P), BF16, ctxC2)
        tf_row = self.sb("c_tf", (1, P), F32, ctxC2)
        cplA = self.sb("c_cplA", (1, P), F32, ctxC2)
        cplB = self.sb("c_cplB", (1, P), F32, ctxC2)
        cplc = self.sb("c_cplc", (P, 2), F32, ctxC2)
        tp2 = self.ps("c_tp2", (P, D), F32, ctxC2)
        h1_ps = self.ps("c_h1", (P, 8, P), F32, ctxC2)
        h2_ps = self.ps("c_h2", (P, 4, P), F32, ctxC2)
        h3_ps = self.ps("c_h3", (P, 512), F32, ctxC2)
        tf_ps = self.ps("c_tfp", (P, 512), F32, ctxC2)
        cplT_ps = self.ps("c_cplT", (P, 512), F32, ctxC2)

        for i in range(NT):
            xa, xkey = self.xtile(i)
            B.op("vector", lambda e, xa=xa, i=i: e.tensor_scalar(
                out=out_cur[:], in0=xa, scalar1=t["negm"][:, i:i + 1],
                scalar2=t["rstd"][:, i:i + 1], op0=OP.add, op1=OP.mult),
                reads=[xkey, (t["negm"], i), (t["rstd"], i)], writes=[out_cur])
            cur, nxt = out_cur, out_nxt
            for k in range(3):
                def tpf(e, cur=cur):
                    for c in range(8):
                        inst = e.transpose(tp2[:, c * P:(c + 1) * P],
                                           cur[:, c * P:(c + 1) * P],
                                           t["identf"][:])
                    return inst
                B.op("tensor", tpf, reads=[cur, t["identf"]], writes=[tp2])
                B.op("vector", lambda e: e.tensor_copy(
                    outT_bf[:, :, :], tp2[:]), reads=[tp2], writes=[outT_bf])

                def h1mm(e):
                    for oc in range(8):
                        e.matmul(h1_ps[:, oc, :],
                                 t["cnb_rows"][:, oc * P:(oc + 1) * P],
                                 t["ones_row"][:],
                                 start=(oc % 4 == 0), stop=False)
                        for c in range(8):
                            inst = e.matmul(
                                h1_ps[:, oc, :],
                                t["cn1_sb"][:, c, oc * P:(oc + 1) * P],
                                outT_bf[:, c, :],
                                start=False, stop=(oc % 4 == 3 and c == 7))
                    return inst
                B.op("tensor", h1mm, reads=[outT_bf, t["cn1_sb"], t["cnb_rows"],
                                            t["ones_row"]], writes=[h1_ps])
                B.op("scalar", lambda e: e.activation(
                    h1T_bf[:, :, :], h1_ps[:, :, :], AF.Gelu),
                    reads=[h1_ps], writes=[h1T_bf])

                def h2mm(e):
                    for oc in range(4):
                        e.matmul(h2_ps[:, oc, :],
                                 t["cnb_rows"][:, D + oc * P:D + (oc + 1) * P],
                                 t["ones_row"][:],
                                 start=(oc == 0), stop=False)
                        for c in range(8):
                            inst = e.matmul(
                                h2_ps[:, oc, :],
                                t["cn2_sb"][:, c, oc * P:(oc + 1) * P],
                                h1T_bf[:, c, :],
                                start=False, stop=(oc == 3 and c == 7))
                    return inst
                B.op("tensor", h2mm, reads=[h1T_bf, t["cn2_sb"], t["cnb_rows"],
                                            t["ones_row"]], writes=[h2_ps])
                B.op("scalar", lambda e: e.activation(
                    h2T_bf[:, :, :], h2_ps[:, :, :], AF.Gelu),
                    reads=[h2_ps], writes=[h2T_bf])

                def h3mm(e):
                    for oc in range(2):
                        e.matmul(h3_ps[:, oc * P:(oc + 1) * P],
                                 t["cnb_rows"][:, D + 512 + oc * P:D + 512 + (oc + 1) * P],
                                 t["ones_row"][:],
                                 start=(oc == 0), stop=False)
                        for c in range(4):
                            inst = e.matmul(
                                h3_ps[:, oc * P:(oc + 1) * P],
                                t["cn3_sb"][:, c, oc * P:(oc + 1) * P],
                                h2T_bf[:, c, :],
                                start=False, stop=(oc == 1 and c == 3))
                    return inst
                B.op("tensor", h3mm, reads=[h2T_bf, t["cn3_sb"], t["cnb_rows"],
                                            t["ones_row"]], writes=[h3_ps])
                B.op("scalar", lambda e: e.activation(
                    h3T_bf[:, :, :], h3_ps[:, 0:256], AF.Gelu),
                    reads=[h3_ps], writes=[h3T_bf])

                def tfmm(e):
                    for c in range(2):
                        inst = e.matmul(tf_ps[0:1, 0:P],
                                        t["cn4_sb"][:, c:c + 1],
                                        h3T_bf[:, c, :],
                                        start=(c == 0), stop=(c == 1))
                    return inst
                B.op("tensor", tfmm, reads=[h3T_bf, t["cn4_sb"]], writes=[tf_ps])
                B.op("scalar", lambda e: e.activation(
                    tf_row[:], tf_ps[0:1, 0:P], AF.Sigmoid, bias=t["cnb4"][:]),
                    reads=[tf_ps, t["cnb4"]], writes=[tf_row])
                B.op("vector", lambda e: e.tensor_scalar(
                    out=cplA[:], in0=tf_row[:], scalar1=cst[0:1, 1:2],
                    scalar2=cst[0:1, 0:1], op0=OP.mult, op1=OP.add),
                    reads=[tf_row, (cst, 0), (cst, 1)], writes=[cplA])
                B.op("vector", lambda e: e.tensor_scalar(
                    out=cplB[:], in0=cplA[:], scalar1=-1.0,
                    scalar2=None, op0=OP.mult),
                    reads=[cplA], writes=[cplB])

                def cpl_tp(e):
                    e.transpose(cplT_ps[0:P, 0:1], cplA[:], t["identf"][0:1, 0:1])
                    return e.transpose(cplT_ps[0:P, 1:2], cplB[:],
                                       t["identf"][0:1, 0:1])
                B.op("tensor", cpl_tp,
                    reads=[cplA, cplB, t["identf"]], writes=[cplT_ps])
                B.op("vector", lambda e: e.tensor_copy(cplc[:], cplT_ps[:, 0:2]),
                    reads=[cplT_ps], writes=[cplc])
                qmap = [(0, 512, 0), (256, 768, 1), (512, 0, 1), (768, 256, 0)]

                def twist(e, cur=cur, nxt=nxt):
                    for (dst, src, neg) in qmap:
                        inst = e.scalar_tensor_tensor(
                            out=nxt[:, dst:dst + 256],
                            in0=cur[:, src:src + 256],
                            scalar=cplc[:, neg:neg + 1],
                            in1=cur[:, dst:dst + 256],
                            op0=OP.mult, op1=OP.add)
                    return inst
                B.op("vector", twist, reads=[cur, cplc], writes=[nxt])
                cur, nxt = nxt, cur
            B.op("vector", lambda e, xa=xa, cur=cur: e.tensor_tensor(
                out=xa, in0=xa, in1=cur[:], op=OP.add),
                reads=[xkey, cur], writes=[xkey])
        self.barrier()
        ctxC2.close()
        ctxC.close()
        if self.cfg.get("tap_l0"):
            self.tap_stream("dbg_l0")

    def phase_D(self):
        B, t = self.B, self.t
        self.B.phase = "D"
        cfg = self.cfg
        NIT = cfg.get("bisect_iters", 36)
        ctxD = ExitStack()
        self.sb("xn3T", (P, 8, NFULL), BF16, ctxD)
        wsem = self.newsem()
        self.sb("wo_sb", (P, 8, 512), BF16, ctxD)
        sy = self.newsem()
        for nm, src in [("bq_c", "bq"), ("bk_c", "bk")]:
            self.sb(nm, (P, 8), F32, ctxD)
            self.load("sync", t[nm][:], self.dram[src][:], t[nm], sy)
        self.sb("bo_bf", (1, D), BF16, ctxD)
        self.sb("bv_bf", (1, D), BF16, ctxD)
        self.load("gpsimd", t["bo_bf"][:], self.dram["bo_row"][:],
                  t["bo_bf"], wsem)
        self.load("gpsimd", t["bv_bf"][:], self.dram["bv_row"][:],
                  t["bv_bf"], wsem)
        self.seal_group(wsem, [t["bo_bf"], t["bv_bf"]])
        self.seal_group(sy, [t["bq_c"], t["bk_c"]])

        ctxD0 = ExitStack()
        self.junk = self.sb("junkD", (P, D), BF16, ctxD0)
        xn3 = [self.sb(f"d_xn{p}", (P, D), BF16, ctxD0) for p in range(2)]
        tp_ps = self.ps("d_tp", (P, D), BF16, ctxD0)
        for i in range(NT):
            par = i % 2
            self.ln_stats(i, t["negm"], t["rstd"])
            self.ln_apply(i, t["negm"], t["rstd"], xn3[par])

            def tpf(e, par=par):
                for c in range(8):
                    inst = e.transpose(tp_ps[:, c * P:(c + 1) * P],
                                       xn3[par][:, c * P:(c + 1) * P],
                                       t["ident"][:])
                return inst
            B.op("tensor", tpf, reads=[xn3[par], t["ident"]], writes=[tp_ps])
            B.op("vector", lambda e, i=i: e.tensor_copy(
                t["xn3T"][:, :, i * P:(i + 1) * P], tp_ps[:].rearrange(
                    "p (c q) -> p c q", q=P)),
                reads=[tp_ps], writes=[(t["xn3T"], i)])
        self.barrier()
        ctxD0.close()
        self.ctx_oth.close()

        half = cfg["half"]
        own_c0 = 16 * half * P
        ctxD1 = ExitStack()
        k2t = self.sb("d_k2t", (P, NFULL), BF16, ctxD1)
        v2 = self.sb("d_v2", (P, 32, P), BF16, ctxD1)
        q2t = self.sb("d_q2t", (P, NOWN // 2), BF16, ctxD1)
        sumv = self.sb("d_sumv", (P, 1), F32, ctxD1)
        s_bfs = [self.sb(f"d_s{p}", (P, NFULL), BF16, ctxD1) for p in range(2)]
        wT = self.sb("d_wT", (P, 32, P), BF16, ctxD1)
        sm = wT.ap().rearrange("p a b -> p (a b)")
        smL = sm[:, 0:2048]
        smR = sm[:, 2048:NFULL]
        wqsl = self.sb("d_wqsl", (P, 8, P), BF16, ctxD1)
        wksl = self.sb("d_wksl", (P, 8, P), BF16, ctxD1)
        wvsl = self.sb("d_wvsl", (P, 8, P), BF16, ctxD1)
        wslsem = [self.newsem(), self.newsem()]
        atnT = self.sb("d_atnT", (P, 8, NOWN // 2), BF16, ctxD1)
        selc = self.sb("d_selc", (P, 48), F32, ctxD1)
        m8 = self.sb("d_m8", (P, 8), F32, ctxD1)
        oh8 = self.sb("d_oh8", (P, 8), F32, ctxD1)
        jk8 = self.sb("d_jk8", (P, 8), F32, ctxD1)
        iota8 = self.sb("d_iota8", (P, 8), F32, ctxD1)
        ninf = self.sb("d_ninf", (P, 1), F32, ctxD1)
        den8 = self.sb("d_den8", (P, 8), F32, ctxD1)
        selm = self.sb("d_selm", (P, 1), mybir.dt.uint8, ctxD1)
        B.op("gpsimd", lambda e: e.memset(t["d_ninf"][:], -1e30),
             writes=[t["d_ninf"]])

        def iota_fill(e):
            for i in range(8):
                inst = e.memset(t["d_iota8"][:, i:i + 1], float(i))
            return inst
        B.op("gpsimd", iota_fill, writes=[t["d_iota8"]])
        rrow = self.sb("d_rrow", (1, P), BF16, ctxD1)
        rb_sb = self.sb("d_rb", (P, P), BF16, ctxD1)
        psP = self.ps("d_psP", (P, 512), F32, ctxD1)
        psQ = self.ps("d_psQ", (P, 512), F32, ctxD1)
        psAV = self.ps("d_psAV", (P, 512), F32, ctxD1)
        psR = self.ps("d_psR", (P, 512), F32, ctxD1)
        psS = self.ps("d_psS", (P, 2048), F32, ctxD1)
        y_ps = psS

        self.B.phase = "D1"
        for mac in range(2):
            mtok0 = mac * (NOWN // 2)
            for hp in range(8):
                wsm = wslsem[hp % 2]
                for nm, src in [("d_wqsl", "Wq"), ("d_wksl", "Wk"),
                                ("d_wvsl", "Wv")]:
                    self.load("gpsimd", t[nm][:, :, :],
                              self.dram[src][hp].rearrange(
                                  "(c p) o -> p c o", p=P), t[nm], wsm)
                self.seal_group(wsm, [t["d_wqsl"], t["d_wksl"], t["d_wvsl"]])
                pp = [psP, psQ]
                for n in range(8):
                    def kproj(e, hp=hp, n=n, ps=pp[n % 2]):
                        for c in range(8):
                            inst = e.matmul(
                                ps[:, 0:512],
                                t["d_wksl"][:, c, :],
                                t["xn3T"][:, c, n * 512:(n + 1) * 512],
                                start=(c == 0), stop=(c == 7))
                        return inst
                    B.op("tensor", kproj, reads=[t["d_wksl"], t["xn3T"]],
                         writes=[pp[n % 2]])
                    B.op("scalar", lambda e, n=n, hp=hp, ps=pp[n % 2]: e.activation(
                        k2t[:, n * 512:(n + 1) * 512], ps[:, 0:512],
                        AF.Identity, bias=t["bk_c"][:, hp:hp + 1]),
                        reads=[pp[n % 2], t["bk_c"]], writes=[(k2t, n)])
                for n in range(32):
                    def vproj(e, hp=hp, n=n, ps=pp[n % 2]):
                        e.matmul(ps[:, 0:P], t["ones_row"][:],
                                 t["bv_bf"][:, hp * P:(hp + 1) * P],
                                 start=True, stop=False)
                        for c in range(8):
                            inst = e.matmul(
                                ps[:, 0:P],
                                t["xn3T"][:, c, n * P:(n + 1) * P],
                                t["d_wvsl"][:, c, :],
                                start=False, stop=(c == 7))
                        return inst
                    B.op("tensor", vproj, reads=[t["d_wvsl"], t["xn3T"],
                                                 t["ones_row"], t["bv_bf"]],
                         writes=[pp[n % 2]])
                    B.op("vector", lambda e, n=n, ps=pp[n % 2]: e.tensor_copy(
                        v2[:, n, :], ps[:, 0:P]),
                        reads=[pp[n % 2]], writes=[(v2, n)])
                for n in range(2):
                    def qproj(e, hp=hp, n=n, mtok0=mtok0, ps=pp[n % 2]):
                        for c in range(8):
                            inst = e.matmul(
                                ps[:, 0:512],
                                t["d_wqsl"][:, c, :],
                                t["xn3T"][:, c, own_c0 + mtok0 + n * 512:
                                          own_c0 + mtok0 + (n + 1) * 512],
                                start=(c == 0), stop=(c == 7))
                        return inst
                    B.op("tensor", qproj, reads=[t["d_wqsl"], t["xn3T"]],
                         writes=[pp[n % 2]])
                    B.op("scalar", lambda e, n=n, hp=hp, ps=pp[n % 2]: e.activation(
                        q2t[:, n * 512:(n + 1) * 512], ps[:, 0:512],
                        AF.Identity, bias=t["bq_c"][:, hp:hp + 1]),
                        reads=[pp[n % 2], t["bq_c"]], writes=[(q2t, n)])
                for tl in range(8):
                    for hh in range(2):
                        for sh in range(2):
                            def smm(e, tl=tl, hh=hh, sh=sh):
                                for n in range(4):
                                    inst = e.matmul(
                                        psS[:, n * 512:(n + 1) * 512],
                                        q2t[64 * hh:64 * hh + 64,
                                            tl * P:(tl + 1) * P],
                                        k2t[64 * hh:64 * hh + 64,
                                            sh * 2048 + n * 512:
                                            sh * 2048 + (n + 1) * 512],
                                        start=True, stop=True,
                                        tile_position=(64 * hh, 0))
                                return inst
                            B.op("tensor", smm, reads=[(q2t, 0), (q2t, 1),
                                                       (k2t, None)],
                                 writes=[psS])
                            B.op("scalar", lambda e, hh=hh, sh=sh: e.activation(
                                s_bfs[hh][:, sh * 2048:(sh + 1) * 2048],
                                psS[:, 0:2048], AF.Identity, scale=0.125,
                                accum_out=selc[:, 24 * hh + 20 + sh:
                                               24 * hh + 21 + sh]),
                                reads=[psS], writes=[(s_bfs[hh], sh),
                                                     (selc, 24 * hh + 20 + sh)])
                    tapsel = (self.cfg.get("tap_sel") is not None
                              and (mac, hp, tl) == tuple(
                                  self.cfg["tap_sel"]))
                    if tapsel:
                        tsem = self.newsem()
                        for nm, src in (("dbg_raw0", s_bfs[0]),
                                        ("dbg_raw1", s_bfs[1])):
                            B.op("gpsimd", lambda e, nm=nm, src=src: e.dma_start(
                                out=self.dram[nm][:, :], in_=src[:]),
                                reads=[(src, 0), (src, 1)], sem=tsem, inc=16)
                        self.final_waits = getattr(self, "final_waits", [])
                        self.final_waits += [(tsem, 32)]
                    self.select_pair(tl, s_bfs, selc, sm, smL, smR, wT, m8,
                                     oh8, jk8, iota8, den8)
                    if tapsel:
                        tsem2 = self.newsem()
                        for nm, src, rds in (
                                ("dbg_selc", selc, [(selc, None)]),
                                ("dbg_den8", den8, [(den8, None)]),
                                ("dbg_w0", s_bfs[0], [(s_bfs[0], 0),
                                                      (s_bfs[0], 1)]),
                                ("dbg_w1", s_bfs[1], [(s_bfs[1], 0),
                                                      (s_bfs[1], 1)])):
                            B.op("gpsimd", lambda e, nm=nm, src=src: e.dma_start(
                                out=self.dram[nm][:, :], in_=src[:]),
                                reads=rds, sem=tsem2, inc=16)
                        self.final_waits += [(tsem2, 64)]
                    for hh in range(2):
                        u = tl * 2 + hh
                        s_bf = s_bfs[hh]
                        dsem = self.newsem() if not hasattr(self, "_dsem") else self._dsem
                        self._dsem = dsem
                        B.op("sync", lambda e, s_bf=s_bf: e.dma_start_transpose(
                            wT[:, :, :], s_bf[:]),
                            reads=[(s_bf, 0), (s_bf, 1)], writes=[wT],
                            sem=dsem, inc=16)

                        def avmm(e, hh=hh):
                            for c in range(32):
                                inst = e.matmul(
                                    psAV[64 * hh:64 * hh + 64, 0:P],
                                    v2[:, c, 64 * hh:64 * hh + 64],
                                    wT[:, c, :],
                                    start=(c == 0), stop=(c == 31),
                                    tile_position=(0, 64 * hh))
                            return inst
                        B.op("tensor", avmm, reads=[v2, wT], writes=[psAV])
                        B.op("tensor", lambda e, u8=u % 8: e.transpose(
                            psR[0:1, 0:P], den8[:, u8:u8 + 1],
                            t["identf"][:]),
                            reads=[(den8, u % 8), t["identf"]], writes=[psR])
                        def rmove(e):
                            with self.nc.allow_low_precision(
                                    reason="den row to bf16"):
                                return e.tensor_scalar(
                                    out=rrow[:], in0=psR[0:1, 0:P],
                                    scalar1=0.0, scalar2=None, op0=OP.add)
                        B.op("vector", rmove, reads=[psR], writes=[rrow])

                        def rrec(e):
                            with self.nc.allow_low_precision(
                                    reason="recip denom row in bf16"):
                                return e.reciprocal(rrow[:], rrow[:])
                        B.op("vector", rrec, reads=[rrow], writes=[rrow])
                        B.op("tensor", lambda e, hh=hh: e.matmul(
                            psR[64 * hh:64 * hh + 64, 0:P],
                            t["ones_row"][:, 0:64], rrow[:],
                            start=True, stop=True, tile_position=(0, 64 * hh)),
                            reads=[rrow, t["ones_row"]], writes=[psR])
                        B.op("vector", lambda e, hh=hh: e.tensor_copy(
                            rb_sb[64 * hh:64 * hh + 64, :],
                            psR[64 * hh:64 * hh + 64, 0:P]), reads=[psR],
                            writes=[rb_sb])
                        B.op("vector", lambda e, hh=hh, hp=hp, tl=tl: e.tensor_tensor(
                            out=atnT[64 * hh:64 * hh + 64, hp,
                                     tl * P:(tl + 1) * P],
                            in0=psAV[64 * hh:64 * hh + 64, 0:P],
                            in1=rb_sb[64 * hh:64 * hh + 64, :],
                            op=OP.mult),
                            reads=[psAV, rb_sb],
                            writes=[(atnT, (hp, tl, hh))])
            if self.cfg.get("tap_atn") == mac:
                asem = self.newsem()
                B.op("gpsimd", lambda e: e.dma_start(
                    out=self.dram["dbg_atnT"][:, :, :], in_=atnT[:]),
                    reads=[(atnT, None)], sem=asem, inc=16)
                self.final_waits = getattr(self, "final_waits", [])
                self.final_waits += [(asem, 16)]
            for oc in range(2):
                self.load("gpsimd", t["wo_sb"][:, :, :],
                          self.dram["Wo"][:, oc * 512:(oc + 1) * 512].rearrange(
                              "(c p) o -> p c o", p=P),
                          t["wo_sb"], wslsem[oc % 2])
                self.seal_group(wslsem[oc % 2], [t["wo_sb"]])
                for tl in range(8):
                    def ymm(e, tl=tl, oc=oc):
                        e.matmul(y_ps[:, tl % 4 * 512:(tl % 4 + 1) * 512],
                                 t["ones_row"][:],
                                 t["bo_bf"][:, oc * 512:(oc + 1) * 512],
                                 start=True, stop=False)
                        for c in range(8):
                            inst = e.matmul(
                                y_ps[:, tl % 4 * 512:(tl % 4 + 1) * 512],
                                atnT[:, c, tl * P:(tl + 1) * P],
                                t["wo_sb"][:, c, :],
                                start=False, stop=(c == 7))
                        return inst
                    B.op("tensor", ymm, reads=[atnT, t["wo_sb"], t["bo_bf"],
                                               t["ones_row"]],
                         writes=[(psS, tl % 4)])
                    ti = 16 * half + mac * 8 + tl
                    xa, xkey = self.xtile(ti)
                    B.op("vector", lambda e, xa=xa, tl=tl, oc=oc: e.tensor_tensor(
                        out=xa[:, oc * 512:(oc + 1) * 512], in0=xa[:, oc * 512:(oc + 1) * 512],
                        in1=y_ps[:, tl % 4 * 512:(tl % 4 + 1) * 512], op=OP.add),
                        reads=[xkey, (psS, tl % 4)], writes=[xkey])
        self.barrier()
        ctxD1.close()
        ctxD.close()
        if self.cfg.get("tap_l1a"):
            rr = self.dram["dbg_l1a"].rearrange("(i p) d -> p i d", p=P)
            ks = self.newsem()
            B.op("sync", lambda e: e.dma_start(
                out=rr[:, :, :], in_=t["x_own"][:, :, :]),
                reads=[t["x_own"]], sem=ks, inc=16)
            self.final_waits = getattr(self, "final_waits", [])
            self.final_waits += [(ks, 16)]

    def select_pair(self, tl, s_bfs, selc, sm, smL, smR, wT, m8, oh8, jk8,
                    iota8, den8, NB=13):
        B, t = self.B, self.t
        U = (0, 1)

        def C(u, i):
            return selc[:, 24 * u + i:24 * u + i + 1]

        def V(fn, reads, writes):
            B.op("vector", fn, reads=reads, writes=writes)

        def ck(u, key):
            return (selc, 24 * u + key)

        def S_(fn, reads, writes):
            B.op("scalar", fn, reads=reads, writes=writes)

        def count(u, thr_i, out_i):
            V(lambda e, u=u: e.tensor_scalar(
                out=smL, in0=s_bfs[u][:, 0:2048], scalar1=C(u, thr_i),
                scalar2=None, op0=OP.is_gt, op1=OP.add,
                accum_out=C(u, 10)),
              [(s_bfs[u], 0), ck(u, thr_i)], [(wT, 0), ck(u, 10)])
            B.op("scalar", lambda e, u=u: e.activation(
                smR, s_bfs[u][:, 2048:NFULL], AF.Sign, bias=C(u, thr_i),
                scale=-1.0, accum_out=C(u, 19)),
                reads=[(s_bfs[u], 1), ck(u, thr_i)],
                writes=[(wT, 1), ck(u, 19)])
            S_(lambda e, u=u: e.activation(
                C(u, out_i), C(u, 19), AF.Identity, scale=-0.5,
                bias=C(u, 10)),
               [ck(u, 19), ck(u, 10)], [ck(u, out_i)])

        TH = 408.5 - 1024.0
        for u in U:
            S_(lambda e, u=u: e.activation(
                sm, s_bfs[u][:], AF.Square, accum_out=C(u, 22)),
               [(s_bfs[u], 0), (s_bfs[u], 1)], [(wT, 0), (wT, 1), ck(u, 22)])
        for u in U:
            S_(lambda e, u=u: e.activation(C(u, 15), C(u, 20), AF.Identity,
                                           bias=C(u, 21)),
               [ck(u, 20), ck(u, 21)], [ck(u, 15)])
            S_(lambda e, u=u: e.activation(C(u, 0), C(u, 15), AF.Identity,
                                           scale=1.0 / NFULL),
               [ck(u, 15)], [ck(u, 0)])
            S_(lambda e, u=u: e.activation(C(u, 1), C(u, 22), AF.Identity,
                                           scale=1.0 / NFULL),
               [ck(u, 22)], [ck(u, 1)])
            S_(lambda e, u=u: e.activation(C(u, 16), C(u, 0), AF.Square),
               [ck(u, 0)], [ck(u, 16)])
            S_(lambda e, u=u: e.activation(C(u, 2), C(u, 16), AF.Identity,
                                           scale=-1.0, bias=C(u, 1)),
               [ck(u, 16), ck(u, 1)], [ck(u, 2)])
            S_(lambda e, u=u: e.activation(C(u, 3), C(u, 2), AF.Sqrt,
                                           bias=t["epscol"][:, 0:1]),
               [ck(u, 2), t["epscol"]], [ck(u, 3)])
        for u in U:
            for col, zz in ((4, 1.215), (5, 1.358), (17, -4.0), (18, 4.0)):
                S_(lambda e, u=u, col=col, zz=zz: e.activation(
                    C(u, col), C(u, 3), AF.Identity, scale=zz,
                    bias=C(u, 0)),
                   [ck(u, 3), ck(u, 0)], [ck(u, col)])
        for u in U:
            count(u, 4, 23)
        for u in U:
            V(lambda e, u=u: e.tensor_scalar(
                out=t["d_selm"][:], in0=C(u, 23), scalar1=TH, scalar2=None,
                op0=OP.is_le), [ck(u, 23)], [t["d_selm"]])
            V(lambda e, u=u: e.copy_predicated(C(u, 4), t["d_selm"][:],
                                               C(u, 17)),
              [t["d_selm"], ck(u, 17)], [ck(u, 4)])
        for u in U:
            count(u, 5, 23)
        for u in U:
            V(lambda e, u=u: e.tensor_scalar(
                out=t["d_selm"][:], in0=C(u, 23), scalar1=TH, scalar2=None,
                op0=OP.is_gt), [ck(u, 23)], [t["d_selm"]])
            V(lambda e, u=u: e.copy_predicated(C(u, 5), t["d_selm"][:],
                                               C(u, 18)),
              [t["d_selm"], ck(u, 18)], [ck(u, 5)])
        for u in U:
            S_(lambda e, u=u: e.activation(C(u, 15), C(u, 4), AF.Identity,
                                           bias=C(u, 5)),
               [ck(u, 4), ck(u, 5)], [ck(u, 15)])
            S_(lambda e, u=u: e.activation(C(u, 6), C(u, 15), AF.Identity,
                                           scale=0.5),
               [ck(u, 15)], [ck(u, 6)])
            S_(lambda e, u=u: e.activation(C(u, 16), C(u, 4), AF.Identity,
                                           scale=-1.0, bias=C(u, 5)),
               [ck(u, 4), ck(u, 5)], [ck(u, 16)])
            S_(lambda e, u=u: e.activation(C(u, 7), C(u, 16), AF.Identity,
                                           scale=0.25),
               [ck(u, 16)], [ck(u, 7)])
            S_(lambda e, u=u: e.activation(C(u, 8), C(u, 16), AF.Identity,
                                           scale=0.5),
               [ck(u, 16)], [ck(u, 8)])
        for k in range(NB):
            for u in U:
                count(u, 6, 23)
            for u in U:
                V(lambda e, u=u: e.tensor_scalar(
                    out=C(u, 9), in0=C(u, 23), scalar1=TH, scalar2=C(u, 8),
                    op0=OP.is_gt, op1=OP.mult),
                  [ck(u, 23), ck(u, 8)], [ck(u, 9)])
                V(lambda e, u=u: e.scalar_tensor_tensor(
                    out=C(u, 6), in0=C(u, 9), scalar=C(u, 7),
                    op0=OP.subtract, in1=C(u, 6), op1=OP.add),
                  [ck(u, 9), ck(u, 7), ck(u, 6)], [ck(u, 6)])
                V(lambda e, u=u: e.tensor_scalar(
                    out=selc[:, 24 * u + 7:24 * u + 9],
                    in0=selc[:, 24 * u + 7:24 * u + 9], scalar1=0.5,
                    scalar2=None, op0=OP.mult),
                  [ck(u, 7), ck(u, 8)], [ck(u, 7), ck(u, 8)])
        for u in U:
            S_(lambda e, u=u: e.activation(C(u, 4), C(u, 8), AF.Identity,
                                           scale=-1.0, bias=C(u, 6)),
               [ck(u, 8), ck(u, 6)], [ck(u, 4)])
        for u in U:
            count(u, 4, 11)
        for u in U:
            V(lambda e, u=u: e.tensor_scalar(
                out=C(u, 12), in0=C(u, 11), scalar1=1024.0 - 409.0,
                scalar2=None, op0=OP.add),
              [ck(u, 11)], [ck(u, 12)])
            V(lambda e, u=u: e.tensor_scalar(
                out=C(u, 12), in0=C(u, 12), scalar1=7.0, scalar2=0.0,
                op0=OP.min, op1=OP.max),
              [ck(u, 12)], [ck(u, 12)])
        for u in U:
            V(lambda e, u=u: e.scalar_tensor_tensor(
                out=sm, in0=s_bfs[u][:], scalar=C(u, 4), op0=OP.is_le,
                in1=t["d_ninf"][:, 0:1].to_broadcast([P, NFULL]),
                op1=OP.mult),
              [(s_bfs[u], 0), (s_bfs[u], 1), ck(u, 4), t["d_ninf"]],
              [(wT, 0), (wT, 1)])
            V(lambda e, u=u: e.scalar_tensor_tensor(
                out=sm, in0=s_bfs[u][:], scalar=-1.0, op0=OP.mult,
                in1=sm, op1=OP.add),
              [(s_bfs[u], 0), (s_bfs[u], 1), (wT, 0), (wT, 1)],
              [(wT, 0), (wT, 1)])
            V(lambda e: e.max(out=m8[:], in_=sm), [(wT, 0), (wT, 1)], [m8])
            V(lambda e, u=u: e.tensor_scalar(out=oh8[:], in0=iota8[:],
                                             scalar1=C(u, 12), scalar2=None,
                                             op0=OP.is_equal),
              [t["d_iota8"], ck(u, 12)], [oh8])
            V(lambda e: e.tensor_tensor(out=oh8[:], in0=oh8[:], in1=m8[:],
                                        op=OP.mult), [oh8, m8], [oh8])
            V(lambda e, u=u: e.tensor_scalar(out=jk8[:], in0=oh8[:],
                                             scalar1=0.0, scalar2=None,
                                             op0=OP.add, op1=OP.add,
                                             accum_out=C(u, 13)),
              [oh8], [jk8, ck(u, 13)])
            V(lambda e, u=u: e.tensor_scalar(out=C(u, 14), in0=C(u, 13),
                                             scalar1=-1.0, scalar2=None,
                                             op0=OP.mult),
              [ck(u, 13)], [ck(u, 14)])
            V(lambda e, u=u: e.scalar_tensor_tensor(
                out=sm, in0=s_bfs[u][:], scalar=C(u, 14), in1=s_bfs[u][:],
                op0=OP.is_ge, op1=OP.mult),
              [(s_bfs[u], 0), (s_bfs[u], 1), ck(u, 14)], [(wT, 0), (wT, 1)])
            B.op("scalar", lambda e, u=u, u8=(tl * 2 + u) % 8: e.activation(
                s_bfs[u][:], sm, AF.Exp, accum_out=den8[:, u8:u8 + 1]),
                reads=[(wT, 0), (wT, 1)],
                writes=[(s_bfs[u], 0), (s_bfs[u], 1), (den8, (tl * 2 + u) % 8)])

    def bisect_unit(self, u8, NIT, s_bf, lo8, hi8, mid8, mtmp, cnt8, msk8,
                    mskn8, sm, smkey):
        B, t = self.B, self.t
        B.op("vector", lambda e, u8=u8: e.memset(mid8[:, u8:u8 + 1], 0.0),
             writes=[(mid8, u8)])
        for k in range(NIT):
            step = 64.0 * (2.0 ** -(k + 1))
            B.op("vector", lambda e, u8=u8: e.tensor_scalar(
                out=sm, in0=s_bf[:], scalar1=mid8[:, u8:u8 + 1],
                scalar2=None, op0=OP.is_gt, op1=OP.add,
                accum_out=cnt8[:, u8:u8 + 1]),
                reads=[(s_bf, 0), (s_bf, 1), (mid8, u8)],
                writes=[smkey, (cnt8, u8)])
            B.op("vector", lambda e, u8=u8, st=step: e.tensor_scalar(
                out=mtmp[:, u8:u8 + 1], in0=cnt8[:, u8:u8 + 1], scalar1=408.5,
                scalar2=2.0 * st, op0=OP.is_gt, op1=OP.mult),
                reads=[(cnt8, u8)], writes=[(mtmp, u8)])
            B.op("vector", lambda e, u8=u8, st=step: e.scalar_tensor_tensor(
                out=mid8[:, u8:u8 + 1], in0=mtmp[:, u8:u8 + 1], scalar=-st,
                in1=mid8[:, u8:u8 + 1], op0=OP.add, op1=OP.add),
                reads=[(mtmp, u8), (mid8, u8)], writes=[(mid8, u8)])
        B.op("vector", lambda e, u8=u8, st=64.0 * (2.0 ** -NIT): e.tensor_scalar(
            out=lo8[:, u8:u8 + 1], in0=mid8[:, u8:u8 + 1], scalar1=-st,
            scalar2=None, op0=OP.add),
            reads=[(mid8, u8)], writes=[(lo8, u8)])

    def phase_E(self):
        B, t = self.B, self.t
        self.B.phase = "E"
        half = self.cfg["half"]
        ctxE = ExitStack()
        self.sb("tT", (P, 8, NOWN), BF16, ctxE)
        self.sb("y_acc", (P, 16, D), BF16, ctxE)
        self.sb("gate_sb", (P, 8, 8), BF16, ctxE)
        self.sb("gateb", (8, 1), F32, ctxE)
        self.sb("ns_row", (1, D), BF16, ctxE)
        self.sb("ns_bc", (P, D), BF16, ctxE)
        self.sb("b2all", (8, D), BF16, ctxE)
        self.sb("W8", (P, 16, 8), F32, ctxE)
        self.sb("weT", (8, 16, P), BF16, ctxE)
        self.sb("rms3", (P, 16), F32, ctxE)
        self.sb("srt8", (P, 8), F32, ctxE)
        self.sb("lg_f", (P, 8), F32, ctxE)
        self.sb("lgT_f", (8, P), F32, ctxE)
        self.sb("p1c", (P, 1), F32, ctxE)
        self.sb("p2c", (P, 1), F32, ctxE)
        self.sb("eqa", (P, 8), F32, ctxE)
        self.sb("eqb", (P, 8), F32, ctxE)
        self.sb("d12", (P, 1), F32, ctxE)
        wsem = self.newsem()
        self.load("gpsimd", t["gate_sb"][:, :, :],
                  self.dram["gate_w"].rearrange("(c p) o -> p c o", p=P),
                  t["gate_sb"], wsem)
        self.load("gpsimd", t["ns_row"][:], self.dram["norm_scale"][:],
                  t["ns_row"], wsem)
        self.load("gpsimd", t["b2all"][:, :], self.dram["mlp2_b"][:, :],
                  t["b2all"], wsem)
        sy2 = self.newsem()
        self.load("sync", t["gateb"][:], self.dram["gate_b"][:], t["gateb"], sy2)
        self.seal_group(wsem, [t["gate_sb"], t["ns_row"], t["b2all"]])

        ctxE0 = ExitStack()
        self.junk = self.sb("junkE", (P, D), BF16, ctxE0)
        tbf = [self.sb(f"e_t{p}", (P, D), BF16, ctxE0) for p in range(2)]
        tp_ps = self.ps("e_tp", (P, D), BF16, ctxE0)
        lg_ps = self.ps("e_lg", (P, 512), F32, ctxE0)
        tg_ps = self.ps("e_tg", (P, 512), F32, ctxE0)
        nb_ps = self.ps("e_nb", (P, 512), F32, ctxE0)

        B.op("tensor", lambda e: e.matmul(
            nb_ps[:, 0:512], t["ones_row"][:], t["ns_row"][:, 0:512],
            start=True, stop=True), reads=[t["ones_row"], t["ns_row"]],
            writes=[nb_ps])
        B.op("vector", lambda e: e.tensor_copy(t["ns_bc"][:, 0:512],
                                               nb_ps[:, 0:512]),
             reads=[nb_ps], writes=[(t["ns_bc"], 0)])
        B.op("tensor", lambda e: e.matmul(
            nb_ps[:, 0:512], t["ones_row"][:], t["ns_row"][:, 512:1024],
            start=True, stop=True), reads=[t["ones_row"], t["ns_row"]],
            writes=[nb_ps])
        B.op("vector", lambda e: e.tensor_copy(t["ns_bc"][:, 512:1024],
                                               nb_ps[:, 0:512]),
             reads=[nb_ps], writes=[(t["ns_bc"], 1)])

        st = t["stat"]
        ubf = [self.sb(f"e_u{p}", (P, D), BF16, ctxE0) for p in range(2)]
        for j in range(16):
            ti = 16 * half + j
            xa, xkey = self.xtile(ti)
            par = j % 2
            c0 = 8 * par
            self.ln_stats(ti, t["negm"], t["rstd"])
            self.ln_apply(ti, t["negm"], t["rstd"], ubf[par])
            B.op("scalar", lambda e, par=par, c0=c0, jk=self.junk: e.activation(
                jk[:], ubf[par][:], AF.Square,
                accum_out=st[:, c0:c0 + 1]),
                reads=[ubf[par]], writes=[(st, c0), self.junk])
            B.op("vector", lambda e, c0=c0: e.tensor_scalar(
                out=st[:, c0 + 1:c0 + 2], in0=st[:, c0:c0 + 1],
                scalar1=1.0 / D, scalar2=None, op0=OP.mult),
                reads=[(st, c0)], writes=[(st, c0 + 1)])
            B.op("scalar", lambda e, c0=c0: e.activation(
                st[:, c0 + 2:c0 + 3], st[:, c0 + 1:c0 + 2], AF.Sqrt,
                bias=t["epscol"][:, 0:1]),
                reads=[(st, c0 + 1), t["epscol"]], writes=[(st, c0 + 2)])
            B.op("vector", lambda e, c0=c0, j=j: e.reciprocal(
                t["rms3"][:, j:j + 1], st[:, c0 + 2:c0 + 3]),
                reads=[(st, c0 + 2)], writes=[(t["rms3"], j)])
            B.op("vector", lambda e, j=j, par=par: e.scalar_tensor_tensor(
                out=tbf[par][:], in0=ubf[par][:], scalar=t["rms3"][:, j:j + 1],
                in1=t["ns_bc"][:], op0=OP.mult, op1=OP.mult),
                reads=[ubf[par], (t["rms3"], j), (t["ns_bc"], 0),
                       (t["ns_bc"], 1)], writes=[tbf[par]])

            def tpf(e, par=par):
                for c in range(8):
                    inst = e.transpose(tp_ps[:, c * P:(c + 1) * P],
                                       tbf[par][:, c * P:(c + 1) * P],
                                       t["ident"][:])
                return inst
            B.op("tensor", tpf, reads=[tbf[par], t["ident"]], writes=[tp_ps])
            B.op("vector", lambda e, j=j: e.tensor_copy(
                t["tT"][:, :, j * P:(j + 1) * P],
                tp_ps[:].rearrange("p (c q) -> p c q", q=P)),
                reads=[tp_ps], writes=[(t["tT"], j)])

            def lgmm(e, j=j):
                for c in range(8):
                    inst = e.matmul(lg_ps[0:8, 0:P], t["gate_sb"][:, c, :],
                                    t["tT"][:, c, j * P:(j + 1) * P],
                                    start=(c == 0), stop=(c == 7))
                return inst
            B.op("tensor", lgmm, reads=[(t["tT"], j), t["gate_sb"]],
                 writes=[lg_ps])
            B.op("scalar", lambda e: e.activation(
                t["lgT_f"][:], lg_ps[0:8, 0:P], AF.Identity,
                bias=t["gateb"][:]),
                reads=[lg_ps, t["gateb"]], writes=[t["lgT_f"]])
            B.op("tensor", lambda e: e.transpose(
                tg_ps[:, 0:8], t["lgT_f"][:], t["identf"][0:8, 0:8]),
                reads=[t["lgT_f"], t["identf"]], writes=[tg_ps])
            B.op("vector", lambda e: e.tensor_copy(t["lg_f"][:], tg_ps[:, 0:8]),
                 reads=[tg_ps], writes=[t["lg_f"]])
            B.op("vector", lambda e: e.max(out=t["srt8"][:], in_=t["lg_f"][:]),
                 reads=[t["lg_f"]], writes=[t["srt8"]])
            B.op("vector", lambda e: e.tensor_tensor(
                out=t["d12"][:], in0=t["srt8"][:, 0:1], in1=t["srt8"][:, 1:2],
                op=OP.subtract), reads=[t["srt8"]], writes=[t["d12"]])
            B.op("scalar", lambda e: e.activation(
                t["p1c"][:], t["d12"][:], AF.Sigmoid),
                reads=[t["d12"]], writes=[t["p1c"]])
            B.op("vector", lambda e: e.tensor_scalar(
                out=t["p2c"][:], in0=t["p1c"][:], scalar1=-1.0, scalar2=1.0,
                op0=OP.mult, op1=OP.add), reads=[t["p1c"]], writes=[t["p2c"]])
            B.op("vector", lambda e: e.tensor_scalar(
                out=t["eqa"][:], in0=t["lg_f"][:], scalar1=t["srt8"][:, 0:1],
                scalar2=None, op0=OP.is_equal),
                reads=[t["lg_f"], t["srt8"]], writes=[t["eqa"]])
            B.op("vector", lambda e: e.tensor_scalar(
                out=t["eqa"][:], in0=t["eqa"][:], scalar1=t["p1c"][:, 0:1],
                scalar2=None, op0=OP.mult),
                reads=[t["eqa"], t["p1c"]], writes=[t["eqa"]])
            B.op("vector", lambda e: e.tensor_scalar(
                out=t["eqb"][:], in0=t["lg_f"][:], scalar1=t["srt8"][:, 1:2],
                scalar2=None, op0=OP.is_equal),
                reads=[t["lg_f"], t["srt8"]], writes=[t["eqb"]])
            B.op("vector", lambda e: e.tensor_scalar(
                out=t["eqb"][:], in0=t["eqb"][:], scalar1=t["p2c"][:, 0:1],
                scalar2=None, op0=OP.mult),
                reads=[t["eqb"], t["p2c"]], writes=[t["eqb"]])
            B.op("vector", lambda e, j=j: e.tensor_tensor(
                out=t["W8"][:, j, :], in0=t["eqa"][:], in1=t["eqb"][:],
                op=OP.add), reads=[t["eqa"], t["eqb"]], writes=[(t["W8"], j)])
            B.op("tensor", lambda e, j=j: e.transpose(
                lg_ps[0:8, 128:256], t["W8"][:, j, :], t["identf"][:]),
                reads=[(t["W8"], j), t["identf"]], writes=[(lg_ps, 1)])
            B.op("vector", lambda e, j=j: e.tensor_copy(
                t["weT"][:, j, :], lg_ps[0:8, 128:256]),
                reads=[(lg_ps, 1)], writes=[(t["weT"], j)])
        self.barrier()
        ctxE0.close()

        ctxE1 = ExitStack()
        w1b = [self.sb(f"e_w1{p}", (P, 8, D), BF16, ctxE1) for p in range(2)]
        w2b = [self.sb(f"e_w2{p}", (P, 4, D), BF16, ctxE1) for p in range(2)]
        b1r = [self.sb(f"e_b1{p}", (1, D), BF16, ctxE1) for p in range(2)]
        xg = self.sb("e_xg", (P, 512), BF16, ctxE1)
        xl = self.sb("e_xl", (P, 512), BF16, ctxE1)
        sg = self.sb("e_sg", (P, 512), BF16, ctxE1)
        t1b = self.sb("e_tb1", (P, 512), BF16, ctxE1)
        swb = self.sb("e_sw", (P, 512), BF16, ctxE1)
        swT = self.sb("e_swT", (P, 4, P), BF16, ctxE1)
        psH = self.ps("e_h", (P, D), F32, ctxE1)
        psH2 = self.ps("e_h2", (P, D), F32, ctxE1)
        psY = self.ps("e_y", (P, D), F32, ctxE1)
        tpE = self.ps("e_tp1", (P, 512), BF16, ctxE1)
        wsems = [self.newsem(), self.newsem()]
        m1 = self.dram["mlp1_w"]
        m2 = self.dram["mlp2_w"]
        m1b = self.dram["mlp1_b"]
        hps = [psH, psH2]
        for ei in range(8):
            for q in range(4):
                par = (ei * 4 + q) % 2
                ws = wsems[par]
                self.load("gpsimd", w1b[par][:, :, :],
                          m1[ei, :, q * D:(q + 1) * D].rearrange(
                              "(c p) o -> p c o", p=P), w1b[par], ws)
                self.load("gpsimd", w2b[par][:, :, :],
                          m2[ei, q * 512:(q + 1) * 512, :].rearrange(
                              "(c p) o -> p c o", p=P), w2b[par], ws)
                self.load("gpsimd", b1r[par][:],
                          m1b[ei:ei + 1, q * D:(q + 1) * D], b1r[par], ws)
                self.seal_group(ws, [w1b[par], w2b[par], b1r[par]])
                def emit_hmm(j, par=par):
                    hp = hps[j % 2]

                    def hmm(e, par=par, j=j, hp=hp):
                        for oc in range(2):
                            e.matmul(hp[:, oc * 512:(oc + 1) * 512],
                                     t["ones_row"][:],
                                     b1r[par][:, oc * 512:(oc + 1) * 512],
                                     start=True, stop=False)
                            for c in range(8):
                                inst = e.matmul(
                                    hp[:, oc * 512:(oc + 1) * 512],
                                    t["tT"][:, c, j * P:(j + 1) * P],
                                    w1b[par][:, c, oc * 512:(oc + 1) * 512],
                                    start=False, stop=(c == 7))
                        return inst
                    B.op("tensor", hmm, reads=[(t["tT"], j), w1b[par],
                                               b1r[par], t["ones_row"]],
                         writes=[hp])
                emit_hmm(0)
                for j in range(16):
                    hp = hps[j % 2]
                    B.op("vector", lambda e, hp=hp: e.tensor_scalar(
                        out=xg[:], in0=hp[:, 0:D:2], scalar1=7.0, scalar2=None,
                        op0=OP.min), reads=[hp], writes=[xg])
                    B.op("vector", lambda e, hp=hp: e.tensor_scalar(
                        out=xl[:], in0=hp[:, 1:D:2], scalar1=7.0, scalar2=-7.0,
                        op0=OP.min, op1=OP.max), reads=[hp], writes=[xl])
                    B.op("scalar", lambda e: e.activation(
                        sg[:], xg[:], AF.Sigmoid, scale=1.702),
                        reads=[xg], writes=[sg])
                    B.op("vector", lambda e: e.scalar_tensor_tensor(
                        out=t1b[:], in0=xl[:], scalar=1.0, in1=sg[:],
                        op0=OP.add, op1=OP.mult), reads=[xl, sg], writes=[t1b])
                    B.op("vector", lambda e, j=j, ei=ei: e.scalar_tensor_tensor(
                        out=swb[:], in0=xg[:],
                        scalar=t["W8"][:, j, ei:ei + 1], in1=t1b[:],
                        op0=OP.mult, op1=OP.mult),
                        reads=[xg, (t["W8"], j), t1b], writes=[swb])
                    if j < 15:
                        emit_hmm(j + 1)

                    def swtp(e):
                        for c in range(4):
                            inst = e.transpose(tpE[:, c * P:(c + 1) * P],
                                               swb[:, c * P:(c + 1) * P],
                                               t["ident"][:])
                        return inst
                    B.op("tensor", swtp, reads=[swb, t["ident"]], writes=[tpE])
                    B.op("vector", lambda e: e.tensor_copy(
                        swT[:, :, :], tpE[:].rearrange("p (c q) -> p c q", q=P)),
                        reads=[tpE], writes=[swT])

                    def ymm(e, par=par):
                        for oc in range(2):
                            for c in range(4):
                                inst = e.matmul(
                                    psY[:, oc * 512:(oc + 1) * 512],
                                    swT[:, c, :],
                                    w2b[par][:, c, oc * 512:(oc + 1) * 512],
                                    start=(c == 0), stop=(c == 3))
                        return inst
                    B.op("tensor", ymm, reads=[swT, w2b[par]], writes=[psY])
                    B.op("vector", lambda e, j=j: e.tensor_tensor(
                        out=t["y_acc"][:, j, :], in0=t["y_acc"][:, j, :],
                        in1=psY[:], op=OP.add),
                        reads=[(t["y_acc"], j), psY],
                        writes=[(t["y_acc"], j)]) if (ei + q > 0) else                     B.op("vector", lambda e, j=j: e.tensor_copy(
                        t["y_acc"][:, j, :], psY[:]),
                        reads=[psY], writes=[(t["y_acc"], j)])
        if self.cfg.get("tap_moe"):
            km = self.newsem()
            B.op("sync", lambda e: e.dma_start(
                out=self.dram["dbg_w8"][:, :, :], in_=t["W8"][:, :, :]),
                reads=[t["W8"]], sem=km, inc=16)
            km2 = self.newsem()
            B.op("gpsimd", lambda e: e.dma_start(
                out=self.dram["dbg_yacc"][:, :, :], in_=t["y_acc"][:, :, :]),
                reads=[t["y_acc"]], sem=km2, inc=16)
            self.final_waits = getattr(self, "final_waits", [])
            self.final_waits += [(km, 16), (km2, 16)]
        utmp = self.sb("e_utmp", (P, D), F32, ctxE1)
        for j in range(16):
            ti = 16 * half + j
            xa, xkey = self.xtile(ti)

            def b2mm(e, j=j):
                for oc in range(2):
                    inst = e.matmul(psY[:, oc * 512:(oc + 1) * 512],
                                    t["weT"][:, j, :],
                                    t["b2all"][:, oc * 512:(oc + 1) * 512],
                                    start=True, stop=True)
                return inst
            B.op("tensor", b2mm, reads=[(t["weT"], j), t["b2all"]],
                 writes=[psY])
            B.op("vector", lambda e, xa=xa, ti=ti: e.tensor_scalar(
                out=utmp[:], in0=xa, scalar1=t["negm"][:, ti:ti + 1],
                scalar2=t["rstd"][:, ti:ti + 1], op0=OP.add, op1=OP.mult),
                reads=[xkey, (t["negm"], ti), (t["rstd"], ti)], writes=[utmp])
            B.op("vector", lambda e, xa=xa: e.tensor_tensor(
                out=xa, in0=xa, in1=utmp[:], op=OP.add),
                reads=[xkey, utmp], writes=[xkey])
            B.op("vector", lambda e, xa=xa, j=j: e.tensor_tensor(
                out=xa, in0=xa, in1=t["y_acc"][:, j, :], op=OP.add),
                reads=[xkey, (t["y_acc"], j)], writes=[xkey])
            B.op("vector", lambda e, xa=xa: e.tensor_tensor(
                out=xa, in0=xa, in1=psY[:], op=OP.add),
                reads=[xkey, psY], writes=[xkey])
        self.barrier()
        ctxE1.close()
        ctxE.close()

    def tap_stream(self, name):
        B, t = self.B, self.t
        half = self.cfg["half"]
        rr = self.dram[name].rearrange("(i p) d -> p i d", p=P)
        ks, kg = self.newsem(), self.newsem()
        B.op("sync", lambda e: e.dma_start(
            out=rr[:, 16 * half:16 * half + 16, :], in_=t["x_own"][:, :, :]),
            reads=[t["x_own"]], sem=ks, inc=16)
        B.op("gpsimd", lambda e: e.dma_start(
            out=rr[:, 16 * (1 - half):16 * (1 - half) + 16, :],
            in_=t["x_oth"][:, :, :]), reads=[t["x_oth"]], sem=kg, inc=16)
        self.final_waits = getattr(self, "final_waits", [])
        self.final_waits += [(ks, 16), (kg, 16)]

    def finalize_output(self):
        B, t = self.B, self.t
        ks = self.newsem()
        B.op("sync", lambda e: e.dma_start(
            out=self.dram["out"].rearrange("(i p) d -> p i d", p=P),
            in_=t["x_own"][:, :, :]), reads=[t["x_own"]], sem=ks, inc=16)
        for s, v in getattr(self, "final_waits", []):
            B.wait("sync", s, v)
        B.wait("sync", ks, 16)

    def build(self):
        self.declare_io()
        self.setup()
        self.phase_A()
        self.phase_B()
        self.phase_C()
        self.phase_D()
        self.phase_E()
        self.finalize_output()
        self.B.emit()
        return self.nc


def host_inputs(inputs, b, half):
    def T(a):
        return np.ascontiguousarray(np.asarray(a).T)

    m = {
        "x": np.ascontiguousarray(np.asarray(inputs["L0_attn_x"][b])),
        "wqkvT": T(inputs["L0_attn_w_qkv"]),
        "woutT": T(inputs["L0_attn_w_out"]),
        "b_out": np.asarray(inputs["L0_attn_b_out"]).reshape(1, D),
        "ar": np.asarray(inputs["L0_ffn_adaptive_range"]).reshape(1, 1),
        "Wq": np.asarray(inputs["L1_attn_Wq"]).reshape(D, 8, P).transpose(1, 0, 2),
        "Wk": np.asarray(inputs["L1_attn_Wk"]).reshape(D, 8, P).transpose(1, 0, 2),
        "Wv": np.asarray(inputs["L1_attn_Wv"]).reshape(D, 8, P).transpose(1, 0, 2),
        "Wo": np.asarray(inputs["L1_attn_Wo"]),
        "bo_row": np.asarray(inputs["L1_attn_bo"]).reshape(1, D),
        "norm_scale": np.asarray(inputs["L1_ffn_norm_scale"]).reshape(1, D),
        "gate_w": np.asarray(inputs["L1_ffn_gate_w"]),
        "gate_b": np.asarray(inputs["L1_ffn_gate_b"]).reshape(8, 1),
        "mlp1_w": np.asarray(inputs["L1_ffn_mlp1_w"]),
        "mlp1_b": np.asarray(inputs["L1_ffn_mlp1_b"]),
        "mlp2_w": np.asarray(inputs["L1_ffn_mlp2_w"]),
        "mlp2_b": np.asarray(inputs["L1_ffn_mlp2_b"]),
    }
    for i in range(1, 4):
        m[f"cn_w{i}"] = np.asarray(inputs[f"L0_ffn_cn_w{i}"])
    for i in range(1, 5):
        m[f"cn_b{i}"] = np.asarray(inputs[f"L0_ffn_cn_b{i}"]).reshape(1, -1)
    m["cn_w4c"] = np.asarray(inputs["L0_ffn_cn_w4"]).reshape(2, P).T
    m["gc_w3c"] = np.asarray(inputs["L0_ffn_gc_w3"]).reshape(2, P).T
    for i in range(1, 3):
        m[f"gc_w{i}"] = np.asarray(inputs[f"L0_ffn_gc_w{i}"])
    m["gc_b1c"] = np.asarray(inputs["L0_ffn_gc_b1"]).reshape(4, P).T
    m["gc_b2c"] = np.asarray(inputs["L0_ffn_gc_b2"]).reshape(2, P).T
    m["gc_b3"] = np.asarray(inputs["L0_ffn_gc_b3"]).reshape(1, 1)
    for nm in ("bq", "bk"):
        m[nm] = np.ascontiguousarray(
            np.asarray(inputs["L1_attn_" + nm]).reshape(8, P).T)
    m["bv_row"] = np.asarray(inputs["L1_attn_bv"]).reshape(1, D)
    return {k: np.ascontiguousarray(v) for k, v in m.items()}


from concourse.bass_utils import run_bass_kernel_spmd

_BUILT = {}


def _get_program():
    if "nc" not in _BUILT:
        m = MK({"half": 0})
        _BUILT["nc"] = m.build()
    return _BUILT["nc"]


def kernel(**inputs):
    nc = _get_program()
    base = host_inputs(inputs, 0, 0)
    x_all = np.asarray(inputs["L0_attn_x"], dtype=np.float32)
    in_maps = []
    for c in range(8):
        b, half = c // 2, c % 2
        im = dict(base)
        xb = x_all[b]
        if half == 1:
            xb = np.concatenate([xb[NOWN:], xb[:NOWN]], axis=0)
        im["x"] = np.ascontiguousarray(xb)
        in_maps.append(im)
    res = run_bass_kernel_spmd(nc, in_maps, core_ids=list(range(8)))
    out = np.empty((4, NFULL, D), dtype=np.float32)
    for c in range(8):
        b, half = c // 2, c % 2
        out[b, half * NOWN:(half + 1) * NOWN] = res.results[c]["out"]
    return out
